# version 17
# baseline (speedup 1.0000x reference)
import contextlib
import numpy as np

import concourse.bass as bass
import concourse.mybir as mybir
from concourse.bass_utils import run_bass_kernel_spmd
from concourse.expressions import RuntimeValue

F32 = mybir.dt.float32
BF16 = mybir.dt.bfloat16
I32 = mybir.dt.int32
AF = mybir.ActivationFunctionType
ALU = mybir.AluOpType
AX = mybir.AxisListType

B, S, D, H, DH = 2, 4096, 2048, 16, 128
HPC = 4
WCOLS = 3 * HPC * DH
XS, NXS = 256, S // 256
QS, NQS = 512, S // 512
TQ = 1024
NCH = 4
NTC, TC = 8, 128
INV_SQRT_DH = float(1.0 / np.sqrt(128.0))
INV_SQRT_R = float(1.0 / np.sqrt(8.0))
EXP_SHIFT = -3.0
LN_EPS = 1e-5

DEBUG = False


class Sched:

    def __init__(self):
        self.count = {}
        self.ops = {e: [] for e in ("sync", "gpsimd", "tensor", "vector", "scalar")}

    def now(self, s):
        return self.count.get(s, 0)

    def add(self, eng, fn, waits=(), incs=()):
        self.ops[eng].append((fn, [w for w in waits if w is not None], list(incs)))
        for s, a in incs:
            self.count[s] = self.count.get(s, 0) + a
        return tuple((s, self.count[s]) for s, a in incs)

    def bump(self, s, a):
        self.count[s] = self.count.get(s, 0) + a

    def emit(self, eng_name, eng, sems):
        seen = {}
        for fn, waits, incs in self.ops[eng_name]:
            for s, v in waits:
                if v > 0 and seen.get(s, -1) < v:
                    eng.wait_ge(sems[s], v)
                    seen[s] = v
            inst = fn(eng)
            for s, a in incs:
                assert inst is not None, f"inc {s} on op with no instruction"
                inst = inst.then_inc(sems[s], a)


def build_nc():
    nc = bass.Bass("TRN2", target_bir_lowering=False)

    xb = nc.dram_tensor("xb", [S, D], F32, kind="ExternalInput").ap()
    xq = nc.dram_tensor("xq", [TQ, D], F32, kind="ExternalInput").ap()
    wqkv = nc.dram_tensor("wqkv", [D, WCOLS], F32, kind="ExternalInput").ap()
    wout = nc.dram_tensor("wout", [D, D], F32, kind="ExternalInput").ap()
    factor = nc.dram_tensor("factor", [D, D], F32, kind="ExternalInput").ap()
    biasv = nc.dram_tensor("biasv", [D], F32, kind="ExternalInput").ap()
    idx = nc.dram_tensor("idx", [4], I32, kind="ExternalInput").ap()
    cblob = nc.dram_tensor("cblob", [128, 2308], BF16, kind="ExternalInput").ap()
    out = nc.dram_tensor("out", [TQ, D], F32, kind="ExternalOutput").ap()
    if DEBUG:
        dbg_ao = nc.dram_tensor("dbg_ao", [128, HPC, S], BF16, kind="ExternalOutput").ap()
        dbg_r1 = nc.dram_tensor("dbg_r1", [128, NTC, D], F32, kind="ExternalOutput").ap()
        dbg_ag = nc.dram_tensor("dbg_ag", [128, 16, TQ], BF16, kind="ExternalOutput").ap()
        dbg_qf = nc.dram_tensor("dbg_qf", [128, HPC, S], BF16, kind="ExternalOutput").ap()
        dbg_kf = nc.dram_tensor("dbg_kf", [128, HPC, S], BF16, kind="ExternalOutput").ap()
        dbg_vt = nc.dram_tensor("dbg_vt", [128, 32, 512], BF16, kind="ExternalOutput").ap()
        dbg_s1 = nc.dram_tensor("dbg_s1", [128, NTC, D], F32, kind="ExternalOutput").ap()
        dbg_r2 = nc.dram_tensor("dbg_r2", [128, NTC, D], F32, kind="ExternalOutput").ap()
        dbg_sf = nc.dram_tensor("dbg_sf", [128, 16, TQ], BF16, kind="ExternalOutput").ap()
        dbg_ws = nc.dram_tensor("dbg_ws", [128, 16, 512], BF16, kind="ExternalOutput").ap()
        dbg_t2 = nc.dram_tensor("dbg_t2", [NTC, 128, D], F32, kind="ExternalOutput").ap()
        dbg_st = nc.dram_tensor("dbg_st", [NTC, 128, 16], F32, kind="ExternalOutput").ap()
        dbg_xq = nc.dram_tensor("dbg_xq", [NTC, 128, D], F32, kind="ExternalOutput").ap()
        dbg_agin = nc.dram_tensor("dbg_agin", [NCH, HPC, 128, TQ], BF16, kind="ExternalOutput").ap()

    ar_in = nc.dram_tensor("ar_in", [D * D], F32, kind="Internal").ap()
    ar_out = nc.dram_tensor("ar_out", [D * D], F32, kind="Internal").ap()
    ag_in = nc.dram_tensor("ag_in", [NCH, HPC, 128, TQ], BF16, kind="Internal").ap()
    ag_out = nc.dram_tensor("ag_out", [NCH, 4, HPC, 128, TQ], BF16, kind="Internal").ap()

    es = contextlib.ExitStack()
    with es:
        sbt = lambda nm, sh, dt: es.enter_context(nc.sbuf_tensor(nm, sh, dt))
        pst = lambda nm, sh, dt: es.enter_context(nc.psum_tensor(nm, sh, dt))

        RA = sbt("RA", [128, 49152], BF16)
        RB = sbt("RB", [128, 24576], BF16)
        RC = sbt("RC", [128, 16384], BF16)
        RD = sbt("RD", [128, 3072], BF16)
        RE = sbt("RE", [128, 8192], BF16)

        q_fm = RA[:, 0:16384].rearrange("p (h t) -> p h t", h=HPC)
        k_fm = RA[:, 16384:32768].rearrange("p (h t) -> p h t", h=HPC)
        v_tm = RA[:, 32768:49152].rearrange("p (c v) -> p c v", c=32)
        wqkv_sb = RB[:, 0:24576].rearrange("p (f c) -> p f c", f=16)
        x_tm = [RC[:, 0:4096].rearrange("p (u d) -> p u d", u=2),
                RC[:, 4096:8192].rearrange("p (u d) -> p u d", u=2)]
        x_fm1 = RC[:, 8192:12288].rearrange("p (f t) -> p f t", f=16)
        x_fm = [x_fm1, x_fm1]
        attnout_sl = RC[:, 12288:16384].rearrange("p (w h t) -> p w h t", w=2, h=HPC)
        pT = [RD[:, 0:512], RD[:, 512:1024], RD[:, 1024:1536], RD[:, 1536:2048],
              RD[:, 2048:2560], RE[:, 3072:3584]]
        bc_sb = RE[:, 2048:3072].bitcast(F32)
        tmpf = RD[:, 2048:3072].bitcast(F32)

        linv = RE[0:1, 2048:3072].bitcast(F32)
        bias_sb = RE[0:1, 4096:8192].bitcast(F32)

        r1s1 = RA[:, 0:32768].bitcast(F32).rearrange("p (c d) -> p c d", c=NTC)
        s1_fm = RA[:, 32768:49152].rearrange("p (f t) -> p f t", f=16)
        ag_sb = RB[:, 0:16384].rearrange("p (f t) -> p f t", f=16)
        s1b = RB[:, 16384:18432]
        outbuf = RB[:, 18432:22528].bitcast(F32)
        w_str = [RC[:, 0:8192].rearrange("p (f c) -> p f c", f=16),
                 RC[:, 8192:16384].rearrange("p (f c) -> p f c", f=16)]
        xq_sb = RE[:, 0:4096].bitcast(F32)
        tmp2 = RE[:, 4096:8192].bitcast(F32)

        CN = sbt("CN", [128, 260], BF16)
        ident = CN[:, 0:128]
        ones_sq = CN[:, 128:256]
        ones_k = CN[:, 256:257]
        masks = RE[:, 0:2048].rearrange("p (j q) -> p j q", j=4)
        ones_rf = sbt("ones_rf", [1, 128], F32)
        eps_t = sbt("eps_t", [128, 1], F32)
        expb_t = sbt("expb_t", [128, 1], F32)
        bias_bc = sbt("bias_bc", [128, D], F32)
        sidx = sbt("sidx", [1, 4], I32)
        stS = sbt("stS", [128, 1], F32)
        stS2 = sbt("stS2", [128, 1], F32)
        stM = sbt("stM", [128, 1], F32)
        stM2 = sbt("stM2", [128, 1], F32)
        stVA = sbt("stVA", [128, 1], F32)
        stV = sbt("stV", [128, 1], F32)
        stSD = sbt("stSD", [128, 1], F32)
        stIV = sbt("stIV", [128, 1], F32)
        stMI = sbt("stMI", [128, 1], F32)
        stNB = sbt("stNB", [128, 1], F32)
        spcA = sbt("spcA", [128, 8], F32)
        spcB = sbt("spcB", [128, 8], F32)

        PT = [pst("PT0", [128, 512], F32), pst("PT1", [128, 512], F32),
              pst("PT2", [128, 512], F32)]
        PA = pst("PA", [128, 512], F32)
        PL = pst("PL", [128, 512], F32)
        PQ = pst("PQ", [128, 512], F32)
        PX = [pst("PX0", [128, 512], BF16), pst("PX1", [128, 512], BF16)]

        semnames = ["sX0", "sX1", "sWQ", "sW0", "sW1", "sTR", "sFM", "sMM", "sEV",
                    "sEXP", "sMSK", "sAOS0", "sAOS1", "sAOS2", "sAOS3", "sCC",
                    "sAG", "sXQ", "sST", "sAR", "sBB", "sLN", "sIDX", "sDB"]
        sems = {n: es.enter_context(nc.semaphore(n)) for n in semnames}

        block = es.enter_context(nc.Block())

        sc = Sched()
        A, N = sc.add, sc.now
        free_at = {}

        FQ = D * D // 4
        for qi, qeng in enumerate(("sync", "scalar", "sync", "scalar")):
            A(qeng, lambda e, qi=qi: e.dma_start(
                out=ar_in[qi * FQ:(qi + 1) * FQ],
                in_=factor.rearrange("a b -> (a b)")[qi * FQ:(qi + 1) * FQ]),
                incs=[("sAR", 16)])
        A("sync", lambda e: e.dma_start(out=bias_sb, in_=biasv[None, :]),
          incs=[("sBB", 16)])
        A("gpsimd", lambda e: e.dma_start(out=sidx[:1], in_=idx[None, :]),
          incs=[("sIDX", 16)])

        def consts_fast(e):
            e.memset(eps_t[:], LN_EPS)
            e.memset(expb_t[:], EXP_SHIFT)
            return e.memset(ones_rf[:], 1.0)
        A("gpsimd", consts_fast, incs=[("sBB", 1)])
        def ldconsts(e):
            e.dma_start(out=CN[:, 0:256], in_=cblob[:, 0:256]).then_inc(sems["sBB"], 16)
            e.dma_start(out=CN[:, 256:257], in_=cblob[:, 2304:2305],
                        allow_slow_non_contiguous=True).then_inc(sems["sBB"], 16)
            return e.dma_start(out=RE[:, 0:2048], in_=cblob[:, 256:2304])
        A("gpsimd", ldconsts, incs=[("sBB", 16)])
        sc.count["sBB"] = sc.count.get("sBB", 0) + 32

        for qd in range(4):
            A("tensor", lambda e, qd=qd: e.matmul(
                out=PQ[:, :], lhsT=ones_rf[0:1, :],
                rhs=bias_sb[0:1, qd * 512:(qd + 1) * 512], start=True, stop=True),
                waits=[("sBB", 65), free_at.get("pq")], incs=[("sMM", 1)])
            ev = A("vector", lambda e, qd=qd: e.tensor_copy(
                out=bias_bc[:, qd * 512:(qd + 1) * 512], in_=PQ[:, :]),
                waits=[("sMM", N("sMM"))], incs=[("sEV", 1)])
            free_at["pq"] = ev[0]

        xevac_end = {}
        xfm_end = {}

        def emit_xload(t):
            sx = "sX0" if t % 2 == 0 else "sX1"
            A("gpsimd", lambda e, t=t: e.dma_start(
                out=x_tm[t % 2],
                in_=xb[t * XS:(t + 1) * XS, :].rearrange("(u p) d -> p u d", p=128)),
                waits=[free_at.get(("xtm", t % 2))], incs=[(sx, 16)])

        def emit_xgroup(t, gi):
            sx = "sX0" if t % 2 == 0 else "sX1"
            if gi < 8:
                grp = gi
                pxb = PX[grp % 2]

                def trf(e, t=t, grp=grp, pxb=pxb):
                    last = None
                    for i in range(4):
                        tile = grp * 4 + i
                        u, fc = tile // 16, tile % 16
                        last = e.transpose(out=pxb[:, i * 128:(i + 1) * 128],
                                           in_=x_tm[t % 2][:, u, fc * 128:(fc + 1) * 128],
                                           identity=ident[:])
                    return last
                A("tensor", trf,
                  waits=[(sx, 16 * (t // 2 + 1)), ("sBB", 65),
                         free_at.get(("px", grp % 2))],
                  incs=[("sTR", 1)])
                tile0 = grp * 4
                u, fc0 = tile0 // 16, tile0 % 16
                ev = A("vector", lambda e, t=t, u=u, fc0=fc0, pxb=pxb: e.tensor_copy(
                    out=x_fm[t % 2][:, fc0:fc0 + 4, u * 128:(u + 1) * 128],
                    in_=pxb[:].rearrange("p (i q) -> p i q", i=4)),
                    waits=[("sTR", N("sTR"))], incs=[("sFM", 1)])
                free_at[("px", grp % 2)] = ev[0]
                if grp == 7:
                    free_at[("xtm", t % 2)] = ("sTR", N("sTR"))
                    xfm_end[t] = N("sFM")
            elif gi < 16:
                cc = gi - 8

                def qkf(e, t=t, cc=cc):
                    last = None
                    for fc in range(16):
                        last = e.matmul(out=PQ[:, 0:XS],
                                        lhsT=wqkv_sb[:, fc, cc * 128:(cc + 1) * 128],
                                        rhs=x_fm[t % 2][:, fc, :],
                                        start=(fc == 0), stop=(fc == 15))
                    return last
                A("tensor", qkf,
                  waits=[("sFM", xfm_end[t]), ("sWQ", 16), free_at.get("pq")],
                  incs=[("sMM", 1)])
                dst = (q_fm if cc < 4 else k_fm)
                hh = cc % 4
                ev = A("vector", lambda e, t=t, dst=dst, hh=hh: e.tensor_copy(
                    out=dst[:, hh, t * XS:(t + 1) * XS], in_=PQ[:, 0:XS]),
                    waits=[("sMM", N("sMM"))], incs=[("sEV", 1)])
                free_at["pq"] = ev[0]
            else:
                u = gi - 16

                def vf(e, t=t, u=u):
                    last = None
                    for fc in range(16):
                        last = e.matmul(out=PQ[:, :],
                                        lhsT=x_fm[t % 2][:, fc, u * 128:(u + 1) * 128],
                                        rhs=wqkv_sb[:, fc, 1024:1536],
                                        start=(fc == 0), stop=(fc == 15))
                    return last
                A("tensor", vf,
                  waits=[("sFM", xfm_end[t]), ("sWQ", 16), free_at.get("pq")],
                  incs=[("sMM", 1)])
                ev = A("vector", lambda e, t=t, u=u: e.tensor_copy(
                    out=v_tm[:, 2 * t + u, :], in_=PQ[:, :]),
                    waits=[("sMM", N("sMM"))], incs=[("sEV", 1)])
                free_at["pq"] = ev[0]


        emit_xload(0)
        A("gpsimd", lambda e: e.dma_start(
            out=wqkv_sb, in_=wqkv.rearrange("(f p) c -> p f c", p=128)),
          incs=[("sWQ", 16)])
        emit_xload(1)
        A("gpsimd", lambda e: e.collective_compute(
            "AllReduce", ALU.add, replica_groups=[[0, 1, 2, 3, 4, 5, 6, 7]],
            ins=[ar_in], outs=[ar_out]),
          waits=[("sAR", 64)], incs=[("sCC", 1)])
        for gi in range(18):
            emit_xgroup(0, gi)
        for gi in range(18):
            emit_xgroup(1, gi)

        LAG = 3
        chunks = []
        for s in range(NQS):
            for h in range(HPC):
                nkc = 4 * s + 4
                for kc in range(nkc):
                    chunks.append((s, h, kc, nkc))
        NCK = len(chunks)
        st_mm = {}
        gate_of = {}
        use_of = {}
        stg_done = {}

        def emit_epilogue(s, h):
            def rec(e):
                e.reciprocal(out=bc_sb[:], in_=PL[:, :])
                return e.tensor_copy(out=spcA[:], in_=spcB[:])
            A("vector", rec, waits=[("sMM", N("sMM"))], incs=[("sEV", 1)])
            free_at["pl"] = ("sEV", N("sEV"))
            waits = [("sMM", N("sMM"))]
            if s >= 2:
                waits.append(stg_done[s - 2])
            ev = A("vector", lambda e, s=s, h=h: e.tensor_mul(
                out=attnout_sl[:, s % 2, h, :], in0=PA[:, :], in1=bc_sb[:]),
                waits=waits, incs=[("sEV", 1)])
            free_at["pa2"] = ev[0]
            if h == HPC - 1:
                c, half = s // 2, s % 2
                saos = f"sAOS{c}"
                for f in range(HPC):
                    A("sync", lambda e, c=c, half=half, f=f, s=s: e.dma_start(
                        out=ag_in[c, f, :, half * QS:(half + 1) * QS],
                        in_=attnout_sl[:, s % 2, f, :]),
                        waits=[("sEV", N("sEV"))], incs=[(saos, 16)])
                stg_done[s] = (saos, N(saos))
                if half == 1:
                    A("gpsimd", lambda e, c=c: e.collective_compute(
                        "AllGather", ALU.bypass,
                        replica_groups=[[0, 1, 2, 3], [4, 5, 6, 7]],
                        ins=[ag_in[c]], outs=[ag_out[c]]),
                        waits=[(saos, N(saos))], incs=[("sCC", 1)])

        def emit_att_iter(n):
            if n >= LAG and n - LAG < NCK:
                m = n - LAG
                s, h, kc, nkc = chunks[m]
                use_i = use_of[m]
                waits = [gate_of[m]]
                if kc == 0:
                    waits += [free_at.get("pa2") or free_at.get("pa"),
                              free_at.get("pl")]

                def avf(e, s=s, h=h, kc=kc, use_i=use_i, nkc=nkc):
                    e.matmul(out=PA[:, :], lhsT=v_tm[:, kc, h * 128:(h + 1) * 128],
                             rhs=pT[use_i][:], start=(kc == 0), stop=(kc == nkc - 1))
                    return e.matmul(out=PL[:, :], lhsT=ones_sq[:, :],
                                    rhs=pT[use_i][:], start=(kc == 0),
                                    stop=(kc == nkc - 1))
                A("tensor", avf, waits=waits, incs=[("sMM", 1)])
                free_at[("pTb", use_i)] = ("sMM", N("sMM"))
                if kc >= 4 * s:
                    free_at[("pTb", m % 3)] = ("sMSK", N("sMSK"))
                if kc == nkc - 1:
                    emit_epilogue(s, h)

            if n < NCK:
                s, h, kc, nkc = chunks[n]
                ptb = PT[n % 3]
                A("tensor", lambda e, s=s, h=h, kc=kc, ptb=ptb: e.matmul(
                    out=ptb[:, :], lhsT=k_fm[:, h, kc * 128:(kc + 1) * 128],
                    rhs=q_fm[:, h, s * QS:(s + 1) * QS], start=True, stop=True),
                    waits=[("sEV", att_gate[s]), free_at.get(("pt2", n % 3))],
                    incs=[("sMM", 1)])
                st_mm[n] = N("sMM")

                pa_i = n % 3
                ex = A("scalar", lambda e, ptb=ptb, pa_i=pa_i: e.activation(
                    out=pT[pa_i][:], in_=ptb[:], func=AF.Exp,
                    bias=expb_t[:], scale=INV_SQRT_DH),
                    waits=[("sMM", st_mm[n]), ("sBB", 65),
                           free_at.get(("pTb", pa_i))],
                    incs=[("sEXP", 1)])
                free_at[("pt2", n % 3)] = ex[0]

                diag = kc >= 4 * s
                if diag:
                    j = kc - 4 * s
                    mb_i = 3 + (n % 3)
                    A("vector", lambda e, pa_i=pa_i, mb_i=mb_i, j=j: e.tensor_mul(
                        out=pT[mb_i][:], in0=pT[pa_i][:], in1=masks[:, j, :]),
                        waits=[("sEXP", N("sEXP")), ("sBB", 65),
                               free_at.get(("pTb", mb_i))],
                        incs=[("sMSK", 1)])
                    use_of[n] = mb_i
                    gate_of[n] = ("sMSK", N("sMSK"))
                else:
                    use_of[n] = pa_i
                    gate_of[n] = ("sEXP", N("sEXP"))

        att_gate = {}
        n = 0
        for s in range(NQS):
            att_gate[s] = xevac_end.get(2 * s + 1, N("sEV"))
            niter = HPC * (4 * s + 4)
            fill = []
            for t in (2 * s + 2, 2 * s + 3):
                if t < NXS:
                    fill.append(("load", t, None))
                    for gi in range(18):
                        fill.append(("grp", t, gi))
            done_f = 0
            COARSE = False
            for i in range(niter):
                want = len(fill) if COARSE else ((i + 1) * len(fill)) // niter
                while done_f < want:
                    kind, t, gi = fill[done_f]
                    if kind == "load":
                        emit_xload(t)
                    else:
                        emit_xgroup(t, gi)
                    done_f += 1
                emit_att_iter(n)
                n += 1
            if 2 * s + 1 + 2 < NXS:
                xevac_end[2 * s + 3] = N("sEV")
        for k in range(LAG):
            emit_att_iter(n)
            n += 1

        phase2_ev_end = N("sEV")
        phase2_msk_end = N("sMSK")



        def rdback(e):
            rg = e.alloc_register("rg")
            e.reg_load(rg, sidx[:1, 0:1])
            vg = RuntimeValue(rg, min_val=0, max_val=NCH - 1)
            e.wait_ge(sems["sCC"], vg + 2)
            last = None
            for r in range(4):
                for f in range(HPC):
                    last = e.dma_start(out=ag_sb[:, 4 * r + f, :],
                                       in_=ag_out[vg, r, f])
                    if not (r == 3 and f == HPC - 1):
                        last.then_inc(sems["sAG"], 16)
            return last
        A("gpsimd", rdback,
          waits=[("sIDX", 16)] + [(f"sAOS{c}", 128) for c in range(NCH)],
          incs=[("sAG", 16)])
        sc.count["sAG"] = 256

        if DEBUG:
            A("sync", lambda e: e.dma_start(out=dbg_ag, in_=ag_sb),
              waits=[("sAG", 256)], incs=[("sAR", 16)])
            A("sync", lambda e: e.dma_start(out=dbg_agin, in_=ag_in),
              waits=[(f"sAOS{c}", 128) for c in range(NCH)], incs=[("sAR", 16)])

        if DEBUG:
            for nm_, src_ in (("dbg_qf", None),):
                pass
            A("sync", lambda e: e.dma_start(out=dbg_qf, in_=q_fm),
              waits=[(f"sAOS{NCH-1}", 128)], incs=[("sAR", 16)])
            A("sync", lambda e: e.dma_start(out=dbg_kf, in_=k_fm),
              waits=[(f"sAOS{NCH-1}", 128)], incs=[("sAR", 16)])
            A("sync", lambda e: e.dma_start(out=dbg_vt, in_=v_tm),
              waits=[(f"sAOS{NCH-1}", 128)], incs=[("sAR", 16)])

        free_at[("pt3", 0)] = free_at.get(("pt2", 0))
        free_at[("pt3", 1)] = free_at.get(("pt2", 1))
        free_at[("pt3", 2)] = free_at.get(("pt2", 2))

        wout_last_ev = {}
        for osp in range(4):
            sw = f"sW{osp % 2}"
            A("gpsimd", lambda e, osp=osp: e.dma_start(
                out=w_str[osp % 2],
                in_=wout.rearrange("(f p) c -> p f c", p=128)[:, :, osp * 512:(osp + 1) * 512]),
                waits=[(f"sAOS{c}", 128) for c in range(NCH)]
                + [free_at.get(("wstr", osp % 2))],
                incs=[(sw, 16)])
            wn = (sw, N(sw))

            for tc in range(NTC):
                pt = PT[tc % 3]

                def woutf(e, osp=osp, tc=tc, pt=pt):
                    last = None
                    for fc in range(16):
                        last = e.matmul(out=pt[:, :],
                                        lhsT=ag_sb[:, fc, tc * 128:(tc + 1) * 128],
                                        rhs=w_str[osp % 2][:, fc, :],
                                        start=(fc == 0), stop=(fc == 15))
                    return last
                A("tensor", woutf,
                  waits=[("sAG", 256), wn, free_at.get(("pt3", tc % 3))],
                  incs=[("sMM", 1)])
                ev = A("vector", lambda e, osp=osp, tc=tc, pt=pt: e.tensor_copy(
                    out=r1s1[:, tc, osp * 512:(osp + 1) * 512], in_=pt[:, :]),
                    waits=[("sMM", N("sMM"))], incs=[("sEV", 1)])
                free_at[("pt3", tc % 3)] = ev[0]
                wout_last_ev[(osp, tc)] = N("sEV")
            free_at[("wstr", osp % 2)] = ("sMM", N("sMM"))

        wout_ev_end = N("sEV")

        if DEBUG:
            A("sync", lambda e: e.dma_start(out=dbg_r1, in_=r1s1),
              waits=[("sEV", wout_ev_end)], incs=[("sAR", 16)])

        s1fm_tc = {}
        r2_last_ev = {}
        for tc in range(NTC):
            A("sync", lambda e, tc=tc: e.dma_start(
                out=xq_sb, in_=xq[tc * TC:(tc + 1) * TC, :]),
                waits=[(f"sAOS{NCH - 1}", 128), free_at.get("xq")],
                incs=[("sXQ", 16)])

            lnw = [("sEV", wout_last_ev[(3, tc)]), ("sXQ", 16 * (tc + 1)), ("sLN", N("sLN"))]
            if DEBUG and tc >= 1:
                lnw.append(("sDB", 192 * tc))

            def ln1v(e, tc=tc):
                e.tensor_add(out=tmp2[:], in0=r1s1[:, tc, :], in1=xq_sb[:])
                e.tensor_mul(out=r1s1[:, tc, :], in0=tmp2[:], in1=tmp2[:])
                e.reduce_sum(out=stS[:], in_=tmp2[:], axis=AX.X)
                e.reduce_sum(out=stS2[:], in_=r1s1[:, tc, :], axis=AX.X)
                e.tensor_scalar_mul(out=stM[:], in0=stS[:], scalar1=1.0 / D)
                e.tensor_scalar_mul(out=stVA[:], in0=stS2[:], scalar1=1.0 / D)
                e.tensor_copy(out=stMI[:], in_=stM[:])
                e.tensor_copy(out=spcA[:], in_=spcB[:])
                e.tensor_mul(out=stM2[:], in0=stM[:], in1=stMI[:])
                e.tensor_copy(out=spcB[:], in_=spcA[:])
                return e.tensor_sub(out=stV[:], in0=stVA[:], in1=stM2[:])
            A("vector", ln1v, waits=lnw, incs=[("sEV", 1)])
            free_at["xq"] = ("sEV", N("sEV"))
            pre = N("sEV")
            if DEBUG:
                A("sync", lambda e, tc=tc: e.dma_start(out=dbg_t2[tc], in_=tmp2[:]),
                  waits=[("sEV", pre)], incs=[("sDB", 16)])
                A("sync", lambda e, tc=tc: e.dma_start(out=dbg_xq[tc], in_=xq_sb[:]),
                  waits=[("sEV", pre)], incs=[("sDB", 16)])

            A("scalar", lambda e: e.activation(
                out=stSD[:], in_=stV[:], func=AF.Sqrt, bias=eps_t[:]),
                waits=[("sEV", pre)], incs=[("sLN", 1)])
            sq_n = N("sLN")

            def ln1ap(e, tc=tc):
                e.reciprocal(out=stIV[:], in_=stSD[:])
                e.tensor_copy(out=spcA[:], in_=spcB[:])
                e.tensor_mul(out=stMI[:], in0=stM[:], in1=stIV[:])
                e.tensor_copy(out=spcB[:], in_=spcA[:])
                e.tensor_scalar_mul(out=stNB[:], in0=stMI[:], scalar1=-1.0)
                e.tensor_copy(out=spcA[:], in_=spcB[:])
                e.tensor_scalar(out=r1s1[:, tc, :], in0=tmp2[:],
                                scalar1=stIV[:], scalar2=stNB[:],
                                op0=ALU.mult, op1=ALU.add)
                return e.tensor_scalar_mul(out=s1b[:], in0=r1s1[:, tc, :],
                                           scalar1=INV_SQRT_R)
            lw = [("sLN", sq_n), free_at.get("s1b")]
            if DEBUG:
                lw.append(("sDB", 192 * tc + 32))
            A("vector", ln1ap, waits=lw, incs=[("sEV", 1)])
            if DEBUG:
                def dumpst(e, tc=tc):
                    last = None
                    for i, t_ in enumerate([stS, stS2, stM, stM2, stVA, stV, stSD, stIV, stMI, stNB]):
                        last = e.dma_start(out=dbg_st[tc, :, i:i + 1], in_=t_[:],
                                           allow_slow_non_contiguous=True)
                        if i < 9:
                            last.then_inc(sems["sDB"], 16)
                    return last
                A("sync", dumpst, waits=[("sEV", N("sEV"))], incs=[("sDB", 16)])
            s1b_ev = N("sEV")

            for grp in range(4):
                pxb = PX[grp % 2]

                def trs(e, tc=tc, grp=grp, pxb=pxb):
                    last = None
                    for i in range(4):
                        fc = grp * 4 + i
                        last = e.transpose(out=pxb[:, i * 128:(i + 1) * 128],
                                           in_=s1b[:, fc * 128:(fc + 1) * 128],
                                           identity=ident[:])
                    return last
                A("tensor", trs,
                  waits=[("sEV", s1b_ev), free_at.get(("px", grp % 2))],
                  incs=[("sTR", 1)])
                ev = A("vector", lambda e, tc=tc, grp=grp, pxb=pxb: e.tensor_copy(
                    out=s1_fm[:, grp * 4:grp * 4 + 4, tc * 128:(tc + 1) * 128],
                    in_=pxb[:].rearrange("p (i q) -> p i q", i=4)),
                    waits=[("sTR", N("sTR"))], incs=[("sFM", 1)])
                free_at[("px", grp % 2)] = ev[0]
            free_at["s1b"] = ("sTR", N("sTR"))
            s1fm_tc[tc] = N("sFM")
        s1fm_done = N("sFM")

        if DEBUG:
            A("sync", lambda e: e.dma_start(out=dbg_s1, in_=r1s1),
              waits=[("sFM", s1fm_done)], incs=[("sAR", 16)])
            A("sync", lambda e: e.dma_start(out=dbg_sf, in_=s1_fm),
              waits=[("sFM", s1fm_done)], incs=[("sAR", 16)])

        for osp in range(4):
            sw = f"sW{osp % 2}"
            A("gpsimd", lambda e, osp=osp: e.dma_start(
                out=w_str[osp % 2],
                in_=ar_out.rearrange("(f p c) -> p f c", p=128, f=16)
                [:, :, osp * 512:(osp + 1) * 512]),
                waits=[("sCC", 1), free_at.get(("wstr", osp % 2))],
                incs=[(sw, 16)])
            wn = (sw, N(sw))

            for tc in range(NTC):
                pt = PT[tc % 3]

                def ffnf(e, osp=osp, tc=tc, pt=pt):
                    last = None
                    for fc in range(16):
                        last = e.matmul(out=pt[:, :],
                                        lhsT=s1_fm[:, fc, tc * 128:(tc + 1) * 128],
                                        rhs=w_str[osp % 2][:, fc, :],
                                        start=(fc == 0), stop=(fc == 15))
                    return last
                A("tensor", ffnf,
                  waits=[("sFM", s1fm_tc[tc]), wn, free_at.get(("pt3", tc % 3))],
                  incs=[("sMM", 1)])

                def r2f(e, osp=osp, tc=tc, pt=pt):
                    e.tensor_add(out=tmpf[:], in0=pt[:, :],
                                 in1=bias_bc[:, osp * 512:(osp + 1) * 512])
                    return e.tensor_add(out=r1s1[:, tc, osp * 512:(osp + 1) * 512],
                                        in0=tmpf[:],
                                        in1=r1s1[:, tc, osp * 512:(osp + 1) * 512])
                def r2f2(e, osp=osp, tc=tc, pt=pt):
                    e.tensor_add(out=tmpf[:], in0=pt[:, :],
                                 in1=bias_bc[:, osp * 512:(osp + 1) * 512])
                    e.tensor_copy(out=spcA[:], in_=spcB[:])
                    e.tensor_add(out=tmp2[:, osp * 512:(osp + 1) * 512],
                                 in0=tmpf[:],
                                 in1=r1s1[:, tc, osp * 512:(osp + 1) * 512])
                    e.tensor_copy(out=spcB[:], in_=spcA[:])
                    return e.tensor_copy(out=r1s1[:, tc, osp * 512:(osp + 1) * 512],
                                         in_=tmp2[:, osp * 512:(osp + 1) * 512])
                A("vector", r2f2, waits=[("sMM", N("sMM"))], incs=[("sEV", 1)])
                free_at[("pt3", tc % 3)] = ("sEV", N("sEV"))
                r2_last_ev[(osp, tc)] = N("sEV")
            free_at[("wstr", osp % 2)] = ("sMM", N("sMM"))

        ffn_ev_end = N("sEV")
        if DEBUG:
            A("sync", lambda e: e.dma_start(out=dbg_r2, in_=r1s1),
              waits=[("sEV", ffn_ev_end)], incs=[("sAR", 16)])
            A("sync", lambda e: e.dma_start(out=dbg_ws, in_=w_str[1]),
              waits=[("sEV", ffn_ev_end)], incs=[("sAR", 16)])

        for tc in range(NTC):
            def ln2v(e, tc=tc):
                e.reduce_sum(out=stS[:], in_=r1s1[:, tc, :], axis=AX.X)
                e.tensor_mul(out=tmp2[:], in0=r1s1[:, tc, :], in1=r1s1[:, tc, :])
                e.reduce_sum(out=stS2[:], in_=tmp2[:], axis=AX.X)
                e.tensor_scalar_mul(out=stM[:], in0=stS[:], scalar1=1.0 / D)
                e.tensor_scalar_mul(out=stVA[:], in0=stS2[:], scalar1=1.0 / D)
                e.tensor_copy(out=stMI[:], in_=stM[:])
                e.tensor_copy(out=spcA[:], in_=spcB[:])
                e.tensor_mul(out=stM2[:], in0=stM[:], in1=stMI[:])
                e.tensor_copy(out=spcB[:], in_=spcA[:])
                return e.tensor_sub(out=stV[:], in0=stVA[:], in1=stM2[:])
            A("vector", ln2v, waits=[("sEV", r2_last_ev[(3, tc)]), ("sLN", N("sLN"))],
              incs=[("sEV", 1)])
            pre = N("sEV")

            A("scalar", lambda e: e.activation(
                out=stSD[:], in_=stV[:], func=AF.Sqrt, bias=eps_t[:]),
                waits=[("sEV", pre)], incs=[("sLN", 1)])
            sq_n = N("sLN")

            waits = [("sLN", sq_n)]
            if tc >= 1:
                waits.append(("sST", 16 * tc))

            def ln2ap(e, tc=tc):
                e.reciprocal(out=stIV[:], in_=stSD[:])
                e.tensor_copy(out=spcA[:], in_=spcB[:])
                e.tensor_mul(out=stMI[:], in0=stM[:], in1=stIV[:])
                e.tensor_copy(out=spcB[:], in_=spcA[:])
                e.tensor_scalar_mul(out=stNB[:], in0=stMI[:], scalar1=-1.0)
                e.tensor_copy(out=spcA[:], in_=spcB[:])
                return e.tensor_scalar(out=outbuf[:], in0=r1s1[:, tc, :],
                                       scalar1=stIV[:], scalar2=stNB[:],
                                       op0=ALU.mult, op1=ALU.add)
            A("vector", ln2ap, waits=waits, incs=[("sEV", 1)])

            A("sync", lambda e, tc=tc: e.dma_start(
                out=out[tc * TC:(tc + 1) * TC, :], in_=outbuf[:]),
                waits=[("sEV", N("sEV"))], incs=[("sST", 16)])

        sst_end = N("sST")
        A("gpsimd", lambda e: e.wait_ge(sems["sST"], sst_end) and None)
        A("sync", lambda e: e.wait_ge(sems["sST"], sst_end) and None)

        @block.sync
        def _(eng):
            sc.emit("sync", eng, sems)

        @block.gpsimd
        def _(eng):
            sc.emit("gpsimd", eng, sems)

        @block.tensor
        def _(eng):
            sc.emit("tensor", eng, sems)

        @block.vector
        def _(eng):
            sc.emit("vector", eng, sems)

        @block.scalar
        def _(eng):
            sc.emit("scalar", eng, sems)

    return nc


_NC_CACHE = None


def kernel(L0_attn_x, L0_attn_w_qkv, L0_attn_w_out, L0_ffn_factors, L0_ffn_bias):
    global _NC_CACHE
    x = np.ascontiguousarray(np.asarray(L0_attn_x, dtype=np.float32))
    w_qkv = np.ascontiguousarray(np.asarray(L0_attn_w_qkv, dtype=np.float32))
    w_out = np.ascontiguousarray(np.asarray(L0_attn_w_out, dtype=np.float32))
    factors = np.ascontiguousarray(np.asarray(L0_ffn_factors, dtype=np.float32))
    bias = np.ascontiguousarray(np.asarray(L0_ffn_bias, dtype=np.float32))

    if _NC_CACHE is None:
        _NC_CACHE = build_nc()
    nc = _NC_CACHE

    import ml_dtypes
    cb = np.zeros((128, 2308), np.float32)
    cb[:, 0:128] = np.eye(128, dtype=np.float32)
    cb[:, 128:256] = 1.0
    kk = np.arange(128)[:, None]
    qq = np.arange(QS)[None, :]
    for j in range(4):
        cb[:, 256 + j * QS:256 + (j + 1) * QS] = (qq >= kk + 128 * j).astype(np.float32)
    cb[:, 2304] = 1.0
    cblob = cb.astype(ml_dtypes.bfloat16)

    in_maps = []
    for c in range(8):
        b, g = c // 4, c % 4
        heads = list(range(HPC * g, HPC * g + HPC))
        cols = []
        for part in range(3):
            for h in heads:
                cols.append(w_qkv[:, part * D + h * DH:part * D + (h + 1) * DH])
        wq = np.ascontiguousarray(np.concatenate(cols, axis=1))
        in_maps.append({
            "xb": x[b],
            "xq": np.ascontiguousarray(x[b, g * TQ:(g + 1) * TQ]),
            "wqkv": wq,
            "wout": w_out,
            "factor": factors[c],
            "biasv": bias,
            "idx": np.array([g, 0, 0, 0], np.int32),
            "cblob": cblob,
        })

    res = run_bass_kernel_spmd(nc, in_maps, core_ids=list(range(8)))
    outp = np.empty((B, S, D), np.float32)
    for c in range(8):
        b, g = c // 4, c % 4
        outp[b, g * TQ:(g + 1) * TQ] = res.results[c]["out"]
    return outp


# version 18
# speedup vs baseline: 1.0053x; 1.0053x over previous
import contextlib
import numpy as np

import concourse.bass as bass
import concourse.mybir as mybir
from concourse.bass_utils import run_bass_kernel_spmd
from concourse.expressions import RuntimeValue

F32 = mybir.dt.float32
BF16 = mybir.dt.bfloat16
I32 = mybir.dt.int32
AF = mybir.ActivationFunctionType
ALU = mybir.AluOpType
AX = mybir.AxisListType

B, S, D, H, DH = 2, 4096, 2048, 16, 128
HPC = 4
WCOLS = 3 * HPC * DH
XS, NXS = 256, S // 256
QS, NQS = 512, S // 512
TQ = 1024
NCH = 4
NTC, TC = 8, 128
INV_SQRT_DH = float(1.0 / np.sqrt(128.0))
INV_SQRT_R = float(1.0 / np.sqrt(8.0))
EXP_SHIFT = -3.0
LN_EPS = 1e-5

DEBUG = False


class Sched:

    def __init__(self):
        self.count = {}
        self.ops = {e: [] for e in ("sync", "gpsimd", "tensor", "vector", "scalar")}

    def now(self, s):
        return self.count.get(s, 0)

    def add(self, eng, fn, waits=(), incs=()):
        self.ops[eng].append((fn, [w for w in waits if w is not None], list(incs)))
        for s, a in incs:
            self.count[s] = self.count.get(s, 0) + a
        return tuple((s, self.count[s]) for s, a in incs)

    def bump(self, s, a):
        self.count[s] = self.count.get(s, 0) + a

    def emit(self, eng_name, eng, sems):
        seen = {}
        for fn, waits, incs in self.ops[eng_name]:
            for s, v in waits:
                if v > 0 and seen.get(s, -1) < v:
                    eng.wait_ge(sems[s], v)
                    seen[s] = v
            inst = fn(eng)
            for s, a in incs:
                assert inst is not None, f"inc {s} on op with no instruction"
                inst = inst.then_inc(sems[s], a)


def build_nc():
    nc = bass.Bass("TRN2", target_bir_lowering=False)

    xb = nc.dram_tensor("xb", [S, D], F32, kind="ExternalInput").ap()
    xq = nc.dram_tensor("xq", [TQ, D], F32, kind="ExternalInput").ap()
    wqkv = nc.dram_tensor("wqkv", [D, WCOLS], F32, kind="ExternalInput").ap()
    wout = nc.dram_tensor("wout", [D, D], F32, kind="ExternalInput").ap()
    factor = nc.dram_tensor("factor", [D, D], F32, kind="ExternalInput").ap()
    biasv = nc.dram_tensor("biasv", [D], F32, kind="ExternalInput").ap()
    idx = nc.dram_tensor("idx", [4], I32, kind="ExternalInput").ap()
    cblob = nc.dram_tensor("cblob", [128, 2308], BF16, kind="ExternalInput").ap()
    out = nc.dram_tensor("out", [TQ, D], F32, kind="ExternalOutput").ap()
    if DEBUG:
        dbg_ao = nc.dram_tensor("dbg_ao", [128, HPC, S], BF16, kind="ExternalOutput").ap()
        dbg_r1 = nc.dram_tensor("dbg_r1", [128, NTC, D], F32, kind="ExternalOutput").ap()
        dbg_ag = nc.dram_tensor("dbg_ag", [128, 16, TQ], BF16, kind="ExternalOutput").ap()
        dbg_qf = nc.dram_tensor("dbg_qf", [128, HPC, S], BF16, kind="ExternalOutput").ap()
        dbg_kf = nc.dram_tensor("dbg_kf", [128, HPC, S], BF16, kind="ExternalOutput").ap()
        dbg_vt = nc.dram_tensor("dbg_vt", [128, 32, 512], BF16, kind="ExternalOutput").ap()
        dbg_s1 = nc.dram_tensor("dbg_s1", [128, NTC, D], F32, kind="ExternalOutput").ap()
        dbg_r2 = nc.dram_tensor("dbg_r2", [128, NTC, D], F32, kind="ExternalOutput").ap()
        dbg_sf = nc.dram_tensor("dbg_sf", [128, 16, TQ], BF16, kind="ExternalOutput").ap()
        dbg_ws = nc.dram_tensor("dbg_ws", [128, 16, 512], BF16, kind="ExternalOutput").ap()
        dbg_t2 = nc.dram_tensor("dbg_t2", [NTC, 128, D], F32, kind="ExternalOutput").ap()
        dbg_st = nc.dram_tensor("dbg_st", [NTC, 128, 16], F32, kind="ExternalOutput").ap()
        dbg_xq = nc.dram_tensor("dbg_xq", [NTC, 128, D], F32, kind="ExternalOutput").ap()
        dbg_agin = nc.dram_tensor("dbg_agin", [NCH, HPC, 128, TQ], BF16, kind="ExternalOutput").ap()

    ar_in = nc.dram_tensor("ar_in", [D * D], F32, kind="Internal").ap()
    ar_out = nc.dram_tensor("ar_out", [D * D], F32, kind="Internal").ap()
    ag_in = nc.dram_tensor("ag_in", [NCH, HPC, 128, TQ], BF16, kind="Internal").ap()
    ag_out = nc.dram_tensor("ag_out", [NCH, 4, HPC, 128, TQ], BF16, kind="Internal").ap()

    es = contextlib.ExitStack()
    with es:
        sbt = lambda nm, sh, dt: es.enter_context(nc.sbuf_tensor(nm, sh, dt))
        pst = lambda nm, sh, dt: es.enter_context(nc.psum_tensor(nm, sh, dt))

        RA = sbt("RA", [128, 49152], BF16)
        RB = sbt("RB", [128, 24576], BF16)
        RC = sbt("RC", [128, 16384], BF16)
        RD = sbt("RD", [128, 3072], BF16)
        RE = sbt("RE", [128, 8192], BF16)

        q_fm = RA[:, 0:16384].rearrange("p (h t) -> p h t", h=HPC)
        k_fm = RA[:, 16384:32768].rearrange("p (h t) -> p h t", h=HPC)
        v_tm = RA[:, 32768:49152].rearrange("p (c v) -> p c v", c=32)
        wqkv_sb = RB[:, 0:24576].rearrange("p (f c) -> p f c", f=16)
        x_tm = [RC[:, 0:4096].rearrange("p (u d) -> p u d", u=2),
                RC[:, 4096:8192].rearrange("p (u d) -> p u d", u=2)]
        x_fm1 = RC[:, 8192:12288].rearrange("p (f t) -> p f t", f=16)
        x_fm = [x_fm1, x_fm1]
        attnout_sl = RC[:, 12288:16384].rearrange("p (w h t) -> p w h t", w=2, h=HPC)
        pT = [RD[:, 0:512], RD[:, 512:1024], RD[:, 1024:1536], RD[:, 1536:2048],
              RD[:, 2048:2560], RE[:, 3072:3584]]
        bc_sb = RE[:, 2048:3072].bitcast(F32)
        tmpf = RD[:, 2048:3072].bitcast(F32)

        linv = RE[0:1, 2048:3072].bitcast(F32)
        bias_sb = RE[0:1, 4096:8192].bitcast(F32)

        r1s1 = RA[:, 0:32768].bitcast(F32).rearrange("p (c d) -> p c d", c=NTC)
        s1_fm = RA[:, 32768:49152].rearrange("p (f t) -> p f t", f=16)
        ag_sb = RB[:, 0:16384].rearrange("p (f t) -> p f t", f=16)
        s1b = RB[:, 16384:18432]
        outbuf = RB[:, 18432:22528].bitcast(F32)
        w_str = [RC[:, 0:8192].rearrange("p (f c) -> p f c", f=16),
                 RC[:, 8192:16384].rearrange("p (f c) -> p f c", f=16)]
        xq_sb = RE[:, 0:4096].bitcast(F32)
        tmp2 = RE[:, 4096:8192].bitcast(F32)

        CN = sbt("CN", [128, 260], BF16)
        ident = CN[:, 0:128]
        ones_sq = CN[:, 128:256]
        ones_k = CN[:, 256:257]
        masks = RE[:, 0:2048].rearrange("p (j q) -> p j q", j=4)
        ones_rf = sbt("ones_rf", [1, 128], F32)
        eps_t = sbt("eps_t", [128, 1], F32)
        expb_t = sbt("expb_t", [128, 1], F32)
        bias_bc = sbt("bias_bc", [128, D], F32)
        sidx = sbt("sidx", [1, 4], I32)
        stS = sbt("stS", [128, 1], F32)
        stS2 = sbt("stS2", [128, 1], F32)
        stM = sbt("stM", [128, 1], F32)
        stM2 = sbt("stM2", [128, 1], F32)
        stVA = sbt("stVA", [128, 1], F32)
        stV = sbt("stV", [128, 1], F32)
        stSD = sbt("stSD", [128, 1], F32)
        stIV = sbt("stIV", [128, 1], F32)
        stMI = sbt("stMI", [128, 1], F32)
        stNB = sbt("stNB", [128, 1], F32)
        spcA = sbt("spcA", [128, 8], F32)
        spcB = sbt("spcB", [128, 8], F32)

        PT = [pst("PT0", [128, 512], F32), pst("PT1", [128, 512], F32),
              pst("PT2", [128, 512], F32)]
        PA = pst("PA", [128, 512], F32)
        PL = pst("PL", [128, 512], F32)
        PQ = pst("PQ", [128, 512], F32)
        PX = [pst("PX0", [128, 512], BF16), pst("PX1", [128, 512], BF16)]

        semnames = ["sX0", "sX1", "sWQ", "sW0", "sW1", "sTR", "sFM", "sMM", "sEV",
                    "sEXP", "sMSK", "sAOS0", "sAOS1", "sAOS2", "sAOS3", "sCC",
                    "sAG", "sXQ", "sST", "sAR", "sBB", "sLN", "sIDX", "sDB"]
        sems = {n: es.enter_context(nc.semaphore(n)) for n in semnames}

        block = es.enter_context(nc.Block())

        sc = Sched()
        A, N = sc.add, sc.now
        free_at = {}

        FQ = D * D // 4
        for qi, qeng in enumerate(("sync", "scalar", "sync", "scalar")):
            A(qeng, lambda e, qi=qi: e.dma_start(
                out=ar_in[qi * FQ:(qi + 1) * FQ],
                in_=factor.rearrange("a b -> (a b)")[qi * FQ:(qi + 1) * FQ]),
                incs=[("sAR", 16)])
        A("sync", lambda e: e.dma_start(out=bias_sb, in_=biasv[None, :]),
          incs=[("sBB", 16)])
        A("gpsimd", lambda e: e.dma_start(out=sidx[:1], in_=idx[None, :]),
          incs=[("sIDX", 16)])

        def consts_fast(e):
            e.memset(eps_t[:], LN_EPS)
            e.memset(expb_t[:], EXP_SHIFT)
            return e.memset(ones_rf[:], 1.0)
        A("gpsimd", consts_fast, incs=[("sBB", 1)])
        def ldconsts(e):
            e.dma_start(out=CN[:, 0:256], in_=cblob[:, 0:256]).then_inc(sems["sBB"], 16)
            e.dma_start(out=CN[:, 256:257], in_=cblob[:, 2304:2305],
                        allow_slow_non_contiguous=True).then_inc(sems["sBB"], 16)
            return e.dma_start(out=RE[:, 0:2048], in_=cblob[:, 256:2304])
        A("gpsimd", ldconsts, incs=[("sBB", 16)])
        sc.count["sBB"] = sc.count.get("sBB", 0) + 32

        for qd in range(4):
            A("tensor", lambda e, qd=qd: e.matmul(
                out=PQ[:, :], lhsT=ones_rf[0:1, :],
                rhs=bias_sb[0:1, qd * 512:(qd + 1) * 512], start=True, stop=True),
                waits=[("sBB", 65), free_at.get("pq")], incs=[("sMM", 1)])
            ev = A("vector", lambda e, qd=qd: e.tensor_copy(
                out=bias_bc[:, qd * 512:(qd + 1) * 512], in_=PQ[:, :]),
                waits=[("sMM", N("sMM"))], incs=[("sEV", 1)])
            free_at["pq"] = ev[0]

        xevac_end = {}
        xfm_end = {}

        def emit_xload(t):
            sx = "sX0" if t % 2 == 0 else "sX1"
            A("gpsimd", lambda e, t=t: e.dma_start(
                out=x_tm[t % 2],
                in_=xb[t * XS:(t + 1) * XS, :].rearrange("(u p) d -> p u d", p=128)),
                waits=[free_at.get(("xtm", t % 2))], incs=[(sx, 16)])

        def emit_xgroup(t, gi):
            sx = "sX0" if t % 2 == 0 else "sX1"
            if gi < 8:
                grp = gi
                pxb = PX[grp % 2]

                def trf(e, t=t, grp=grp, pxb=pxb):
                    last = None
                    for i in range(4):
                        tile = grp * 4 + i
                        u, fc = tile // 16, tile % 16
                        last = e.transpose(out=pxb[:, i * 128:(i + 1) * 128],
                                           in_=x_tm[t % 2][:, u, fc * 128:(fc + 1) * 128],
                                           identity=ident[:])
                    return last
                A("tensor", trf,
                  waits=[(sx, 16 * (t // 2 + 1)), ("sBB", 65),
                         free_at.get(("px", grp % 2))],
                  incs=[("sTR", 1)])
                tile0 = grp * 4
                u, fc0 = tile0 // 16, tile0 % 16
                ev = A("vector", lambda e, t=t, u=u, fc0=fc0, pxb=pxb: e.tensor_copy(
                    out=x_fm[t % 2][:, fc0:fc0 + 4, u * 128:(u + 1) * 128],
                    in_=pxb[:].rearrange("p (i q) -> p i q", i=4)),
                    waits=[("sTR", N("sTR"))], incs=[("sFM", 1)])
                free_at[("px", grp % 2)] = ev[0]
                if grp == 7:
                    free_at[("xtm", t % 2)] = ("sTR", N("sTR"))
                    xfm_end[t] = N("sFM")
            elif gi < 16:
                cc = gi - 8

                def qkf(e, t=t, cc=cc):
                    last = None
                    for fc in range(16):
                        last = e.matmul(out=PQ[:, 0:XS],
                                        lhsT=wqkv_sb[:, fc, cc * 128:(cc + 1) * 128],
                                        rhs=x_fm[t % 2][:, fc, :],
                                        start=(fc == 0), stop=(fc == 15))
                    return last
                A("tensor", qkf,
                  waits=[("sFM", xfm_end[t]), ("sWQ", 16), free_at.get("pq")],
                  incs=[("sMM", 1)])
                dst = (q_fm if cc < 4 else k_fm)
                hh = cc % 4
                ev = A("vector", lambda e, t=t, dst=dst, hh=hh: e.tensor_copy(
                    out=dst[:, hh, t * XS:(t + 1) * XS], in_=PQ[:, 0:XS]),
                    waits=[("sMM", N("sMM"))], incs=[("sEV", 1)])
                free_at["pq"] = ev[0]
            else:
                u = gi - 16

                def vf(e, t=t, u=u):
                    last = None
                    for fc in range(16):
                        last = e.matmul(out=PQ[:, :],
                                        lhsT=x_fm[t % 2][:, fc, u * 128:(u + 1) * 128],
                                        rhs=wqkv_sb[:, fc, 1024:1536],
                                        start=(fc == 0), stop=(fc == 15))
                    return last
                A("tensor", vf,
                  waits=[("sFM", xfm_end[t]), ("sWQ", 16), free_at.get("pq")],
                  incs=[("sMM", 1)])
                ev = A("vector", lambda e, t=t, u=u: e.tensor_copy(
                    out=v_tm[:, 2 * t + u, :], in_=PQ[:, :]),
                    waits=[("sMM", N("sMM"))], incs=[("sEV", 1)])
                free_at["pq"] = ev[0]


        emit_xload(0)
        A("gpsimd", lambda e: e.dma_start(
            out=wqkv_sb, in_=wqkv.rearrange("(f p) c -> p f c", p=128)),
          incs=[("sWQ", 16)])
        emit_xload(1)
        for gi in range(18):
            emit_xgroup(0, gi)
        for gi in range(18):
            emit_xgroup(1, gi)

        LAG = 3
        chunks = []
        for s in range(NQS):
            for h in range(HPC):
                nkc = 4 * s + 4
                for kc in range(nkc):
                    chunks.append((s, h, kc, nkc))
        NCK = len(chunks)
        st_mm = {}
        gate_of = {}
        use_of = {}
        stg_done = {}

        def emit_epilogue(s, h):
            def rec(e):
                e.reciprocal(out=bc_sb[:], in_=PL[:, :])
                return e.tensor_copy(out=spcA[:], in_=spcB[:])
            A("vector", rec, waits=[("sMM", N("sMM"))], incs=[("sEV", 1)])
            free_at["pl"] = ("sEV", N("sEV"))
            waits = [("sMM", N("sMM"))]
            if s >= 2:
                waits.append(stg_done[s - 2])
            ev = A("vector", lambda e, s=s, h=h: e.tensor_mul(
                out=attnout_sl[:, s % 2, h, :], in0=PA[:, :], in1=bc_sb[:]),
                waits=waits, incs=[("sEV", 1)])
            free_at["pa2"] = ev[0]
            if h == HPC - 1:
                c, half = s // 2, s % 2
                saos = f"sAOS{c}"
                for f in range(HPC):
                    A("sync", lambda e, c=c, half=half, f=f, s=s: e.dma_start(
                        out=ag_in[c, f, :, half * QS:(half + 1) * QS],
                        in_=attnout_sl[:, s % 2, f, :]),
                        waits=[("sEV", N("sEV"))], incs=[(saos, 16)])
                stg_done[s] = (saos, N(saos))
                if half == 1:
                    A("gpsimd", lambda e, c=c: e.collective_compute(
                        "AllGather", ALU.bypass,
                        replica_groups=[[0, 1, 2, 3], [4, 5, 6, 7]],
                        ins=[ag_in[c]], outs=[ag_out[c]]),
                        waits=[(saos, N(saos))], incs=[("sCC", 1)])

        def emit_att_iter(n):
            if n >= LAG and n - LAG < NCK:
                m = n - LAG
                s, h, kc, nkc = chunks[m]
                use_i = use_of[m]
                waits = [gate_of[m]]
                if kc == 0:
                    waits += [free_at.get("pa2") or free_at.get("pa"),
                              free_at.get("pl")]

                def avf(e, s=s, h=h, kc=kc, use_i=use_i, nkc=nkc):
                    e.matmul(out=PA[:, :], lhsT=v_tm[:, kc, h * 128:(h + 1) * 128],
                             rhs=pT[use_i][:], start=(kc == 0), stop=(kc == nkc - 1))
                    return e.matmul(out=PL[:, :], lhsT=ones_sq[:, :],
                                    rhs=pT[use_i][:], start=(kc == 0),
                                    stop=(kc == nkc - 1))
                A("tensor", avf, waits=waits, incs=[("sMM", 1)])
                free_at[("pTb", use_i)] = ("sMM", N("sMM"))
                if kc >= 4 * s:
                    free_at[("pTb", m % 3)] = ("sMSK", N("sMSK"))
                if kc == nkc - 1:
                    emit_epilogue(s, h)

            if n < NCK:
                s, h, kc, nkc = chunks[n]
                ptb = PT[n % 3]
                A("tensor", lambda e, s=s, h=h, kc=kc, ptb=ptb: e.matmul(
                    out=ptb[:, :], lhsT=k_fm[:, h, kc * 128:(kc + 1) * 128],
                    rhs=q_fm[:, h, s * QS:(s + 1) * QS], start=True, stop=True),
                    waits=[("sEV", att_gate[s]), free_at.get(("pt2", n % 3))],
                    incs=[("sMM", 1)])
                st_mm[n] = N("sMM")

                pa_i = n % 3
                ex = A("scalar", lambda e, ptb=ptb, pa_i=pa_i: e.activation(
                    out=pT[pa_i][:], in_=ptb[:], func=AF.Exp,
                    bias=expb_t[:], scale=INV_SQRT_DH),
                    waits=[("sMM", st_mm[n]), ("sBB", 65),
                           free_at.get(("pTb", pa_i))],
                    incs=[("sEXP", 1)])
                free_at[("pt2", n % 3)] = ex[0]

                diag = kc >= 4 * s
                if diag:
                    j = kc - 4 * s
                    mb_i = 3 + (n % 3)
                    A("vector", lambda e, pa_i=pa_i, mb_i=mb_i, j=j: e.tensor_mul(
                        out=pT[mb_i][:], in0=pT[pa_i][:], in1=masks[:, j, :]),
                        waits=[("sEXP", N("sEXP")), ("sBB", 65),
                               free_at.get(("pTb", mb_i))],
                        incs=[("sMSK", 1)])
                    use_of[n] = mb_i
                    gate_of[n] = ("sMSK", N("sMSK"))
                else:
                    use_of[n] = pa_i
                    gate_of[n] = ("sEXP", N("sEXP"))

        att_gate = {}
        n = 0
        for s in range(NQS):
            if s == 1:
                A("gpsimd", lambda e: e.collective_compute(
                    "AllReduce", ALU.add, replica_groups=[[0, 1, 2, 3, 4, 5, 6, 7]],
                    ins=[ar_in], outs=[ar_out]),
                  waits=[("sAR", 64)], incs=[("sCC", 1)])
            att_gate[s] = xevac_end.get(2 * s + 1, N("sEV"))
            niter = HPC * (4 * s + 4)
            fill = []
            for t in (2 * s + 2, 2 * s + 3):
                if t < NXS:
                    fill.append(("load", t, None))
                    for gi in range(18):
                        fill.append(("grp", t, gi))
            done_f = 0
            COARSE = False
            for i in range(niter):
                want = len(fill) if COARSE else ((i + 1) * len(fill)) // niter
                while done_f < want:
                    kind, t, gi = fill[done_f]
                    if kind == "load":
                        emit_xload(t)
                    else:
                        emit_xgroup(t, gi)
                    done_f += 1
                emit_att_iter(n)
                n += 1
            if 2 * s + 1 + 2 < NXS:
                xevac_end[2 * s + 3] = N("sEV")
        for k in range(LAG):
            emit_att_iter(n)
            n += 1

        phase2_ev_end = N("sEV")
        phase2_msk_end = N("sMSK")



        def rdback(e):
            rg = e.alloc_register("rg")
            e.reg_load(rg, sidx[:1, 0:1])
            vg = RuntimeValue(rg, min_val=0, max_val=NCH - 1)
            e.wait_ge(sems["sCC"], vg + 2)
            last = None
            for r in range(4):
                for f in range(HPC):
                    last = e.dma_start(out=ag_sb[:, 4 * r + f, :],
                                       in_=ag_out[vg, r, f])
                    if not (r == 3 and f == HPC - 1):
                        last.then_inc(sems["sAG"], 16)
            return last
        A("gpsimd", rdback,
          waits=[("sIDX", 16)] + [(f"sAOS{c}", 128) for c in range(NCH)],
          incs=[("sAG", 16)])
        sc.count["sAG"] = 256

        if DEBUG:
            A("sync", lambda e: e.dma_start(out=dbg_ag, in_=ag_sb),
              waits=[("sAG", 256)], incs=[("sAR", 16)])
            A("sync", lambda e: e.dma_start(out=dbg_agin, in_=ag_in),
              waits=[(f"sAOS{c}", 128) for c in range(NCH)], incs=[("sAR", 16)])

        if DEBUG:
            for nm_, src_ in (("dbg_qf", None),):
                pass
            A("sync", lambda e: e.dma_start(out=dbg_qf, in_=q_fm),
              waits=[(f"sAOS{NCH-1}", 128)], incs=[("sAR", 16)])
            A("sync", lambda e: e.dma_start(out=dbg_kf, in_=k_fm),
              waits=[(f"sAOS{NCH-1}", 128)], incs=[("sAR", 16)])
            A("sync", lambda e: e.dma_start(out=dbg_vt, in_=v_tm),
              waits=[(f"sAOS{NCH-1}", 128)], incs=[("sAR", 16)])

        free_at[("pt3", 0)] = free_at.get(("pt2", 0))
        free_at[("pt3", 1)] = free_at.get(("pt2", 1))
        free_at[("pt3", 2)] = free_at.get(("pt2", 2))

        wout_last_ev = {}
        for osp in range(4):
            sw = f"sW{osp % 2}"
            A("gpsimd", lambda e, osp=osp: e.dma_start(
                out=w_str[osp % 2],
                in_=wout.rearrange("(f p) c -> p f c", p=128)[:, :, osp * 512:(osp + 1) * 512]),
                waits=[(f"sAOS{c}", 128) for c in range(NCH)]
                + [free_at.get(("wstr", osp % 2))],
                incs=[(sw, 16)])
            wn = (sw, N(sw))

            for tc in range(NTC):
                pt = PT[tc % 3]

                def woutf(e, osp=osp, tc=tc, pt=pt):
                    last = None
                    for fc in range(16):
                        last = e.matmul(out=pt[:, :],
                                        lhsT=ag_sb[:, fc, tc * 128:(tc + 1) * 128],
                                        rhs=w_str[osp % 2][:, fc, :],
                                        start=(fc == 0), stop=(fc == 15))
                    return last
                A("tensor", woutf,
                  waits=[("sAG", 256), wn, free_at.get(("pt3", tc % 3))],
                  incs=[("sMM", 1)])
                ev = A("vector", lambda e, osp=osp, tc=tc, pt=pt: e.tensor_copy(
                    out=r1s1[:, tc, osp * 512:(osp + 1) * 512], in_=pt[:, :]),
                    waits=[("sMM", N("sMM"))], incs=[("sEV", 1)])
                free_at[("pt3", tc % 3)] = ev[0]
                wout_last_ev[(osp, tc)] = N("sEV")
            free_at[("wstr", osp % 2)] = ("sMM", N("sMM"))

        wout_ev_end = N("sEV")

        if DEBUG:
            A("sync", lambda e: e.dma_start(out=dbg_r1, in_=r1s1),
              waits=[("sEV", wout_ev_end)], incs=[("sAR", 16)])

        s1fm_tc = {}
        r2_last_ev = {}
        for tc in range(NTC):
            A("sync", lambda e, tc=tc: e.dma_start(
                out=xq_sb, in_=xq[tc * TC:(tc + 1) * TC, :]),
                waits=[(f"sAOS{NCH - 1}", 128), free_at.get("xq")],
                incs=[("sXQ", 16)])

            lnw = [("sEV", wout_last_ev[(3, tc)]), ("sXQ", 16 * (tc + 1)), ("sLN", N("sLN"))]
            if DEBUG and tc >= 1:
                lnw.append(("sDB", 192 * tc))

            def ln1v(e, tc=tc):
                e.tensor_add(out=tmp2[:], in0=r1s1[:, tc, :], in1=xq_sb[:])
                e.tensor_mul(out=r1s1[:, tc, :], in0=tmp2[:], in1=tmp2[:])
                e.reduce_sum(out=stS[:], in_=tmp2[:], axis=AX.X)
                e.reduce_sum(out=stS2[:], in_=r1s1[:, tc, :], axis=AX.X)
                e.tensor_scalar_mul(out=stM[:], in0=stS[:], scalar1=1.0 / D)
                e.tensor_scalar_mul(out=stVA[:], in0=stS2[:], scalar1=1.0 / D)
                e.tensor_copy(out=stMI[:], in_=stM[:])
                e.tensor_copy(out=spcA[:], in_=spcB[:])
                e.tensor_mul(out=stM2[:], in0=stM[:], in1=stMI[:])
                e.tensor_copy(out=spcB[:], in_=spcA[:])
                return e.tensor_sub(out=stV[:], in0=stVA[:], in1=stM2[:])
            A("vector", ln1v, waits=lnw, incs=[("sEV", 1)])
            free_at["xq"] = ("sEV", N("sEV"))
            pre = N("sEV")
            if DEBUG:
                A("sync", lambda e, tc=tc: e.dma_start(out=dbg_t2[tc], in_=tmp2[:]),
                  waits=[("sEV", pre)], incs=[("sDB", 16)])
                A("sync", lambda e, tc=tc: e.dma_start(out=dbg_xq[tc], in_=xq_sb[:]),
                  waits=[("sEV", pre)], incs=[("sDB", 16)])

            A("scalar", lambda e: e.activation(
                out=stSD[:], in_=stV[:], func=AF.Sqrt, bias=eps_t[:]),
                waits=[("sEV", pre)], incs=[("sLN", 1)])
            sq_n = N("sLN")

            def ln1ap(e, tc=tc):
                e.reciprocal(out=stIV[:], in_=stSD[:])
                e.tensor_copy(out=spcA[:], in_=spcB[:])
                e.tensor_mul(out=stMI[:], in0=stM[:], in1=stIV[:])
                e.tensor_copy(out=spcB[:], in_=spcA[:])
                e.tensor_scalar_mul(out=stNB[:], in0=stMI[:], scalar1=-1.0)
                e.tensor_copy(out=spcA[:], in_=spcB[:])
                e.tensor_scalar(out=r1s1[:, tc, :], in0=tmp2[:],
                                scalar1=stIV[:], scalar2=stNB[:],
                                op0=ALU.mult, op1=ALU.add)
                return e.tensor_scalar_mul(out=s1b[:], in0=r1s1[:, tc, :],
                                           scalar1=INV_SQRT_R)
            lw = [("sLN", sq_n), free_at.get("s1b")]
            if DEBUG:
                lw.append(("sDB", 192 * tc + 32))
            A("vector", ln1ap, waits=lw, incs=[("sEV", 1)])
            if DEBUG:
                def dumpst(e, tc=tc):
                    last = None
                    for i, t_ in enumerate([stS, stS2, stM, stM2, stVA, stV, stSD, stIV, stMI, stNB]):
                        last = e.dma_start(out=dbg_st[tc, :, i:i + 1], in_=t_[:],
                                           allow_slow_non_contiguous=True)
                        if i < 9:
                            last.then_inc(sems["sDB"], 16)
                    return last
                A("sync", dumpst, waits=[("sEV", N("sEV"))], incs=[("sDB", 16)])
            s1b_ev = N("sEV")

            for grp in range(4):
                pxb = PX[grp % 2]

                def trs(e, tc=tc, grp=grp, pxb=pxb):
                    last = None
                    for i in range(4):
                        fc = grp * 4 + i
                        last = e.transpose(out=pxb[:, i * 128:(i + 1) * 128],
                                           in_=s1b[:, fc * 128:(fc + 1) * 128],
                                           identity=ident[:])
                    return last
                A("tensor", trs,
                  waits=[("sEV", s1b_ev), free_at.get(("px", grp % 2))],
                  incs=[("sTR", 1)])
                ev = A("vector", lambda e, tc=tc, grp=grp, pxb=pxb: e.tensor_copy(
                    out=s1_fm[:, grp * 4:grp * 4 + 4, tc * 128:(tc + 1) * 128],
                    in_=pxb[:].rearrange("p (i q) -> p i q", i=4)),
                    waits=[("sTR", N("sTR"))], incs=[("sFM", 1)])
                free_at[("px", grp % 2)] = ev[0]
            free_at["s1b"] = ("sTR", N("sTR"))
            s1fm_tc[tc] = N("sFM")
        s1fm_done = N("sFM")

        if DEBUG:
            A("sync", lambda e: e.dma_start(out=dbg_s1, in_=r1s1),
              waits=[("sFM", s1fm_done)], incs=[("sAR", 16)])
            A("sync", lambda e: e.dma_start(out=dbg_sf, in_=s1_fm),
              waits=[("sFM", s1fm_done)], incs=[("sAR", 16)])

        for osp in range(4):
            sw = f"sW{osp % 2}"
            A("gpsimd", lambda e, osp=osp: e.dma_start(
                out=w_str[osp % 2],
                in_=ar_out.rearrange("(f p c) -> p f c", p=128, f=16)
                [:, :, osp * 512:(osp + 1) * 512]),
                waits=[("sCC", 1), free_at.get(("wstr", osp % 2))],
                incs=[(sw, 16)])
            wn = (sw, N(sw))

            for tc in range(NTC):
                pt = PT[tc % 3]

                def ffnf(e, osp=osp, tc=tc, pt=pt):
                    last = None
                    for fc in range(16):
                        last = e.matmul(out=pt[:, :],
                                        lhsT=s1_fm[:, fc, tc * 128:(tc + 1) * 128],
                                        rhs=w_str[osp % 2][:, fc, :],
                                        start=(fc == 0), stop=(fc == 15))
                    return last
                A("tensor", ffnf,
                  waits=[("sFM", s1fm_tc[tc]), wn, free_at.get(("pt3", tc % 3))],
                  incs=[("sMM", 1)])

                def r2f(e, osp=osp, tc=tc, pt=pt):
                    e.tensor_add(out=tmpf[:], in0=pt[:, :],
                                 in1=bias_bc[:, osp * 512:(osp + 1) * 512])
                    return e.tensor_add(out=r1s1[:, tc, osp * 512:(osp + 1) * 512],
                                        in0=tmpf[:],
                                        in1=r1s1[:, tc, osp * 512:(osp + 1) * 512])
                def r2f2(e, osp=osp, tc=tc, pt=pt):
                    e.tensor_add(out=tmpf[:], in0=pt[:, :],
                                 in1=bias_bc[:, osp * 512:(osp + 1) * 512])
                    e.tensor_copy(out=spcA[:], in_=spcB[:])
                    e.tensor_add(out=tmp2[:, osp * 512:(osp + 1) * 512],
                                 in0=tmpf[:],
                                 in1=r1s1[:, tc, osp * 512:(osp + 1) * 512])
                    e.tensor_copy(out=spcB[:], in_=spcA[:])
                    return e.tensor_copy(out=r1s1[:, tc, osp * 512:(osp + 1) * 512],
                                         in_=tmp2[:, osp * 512:(osp + 1) * 512])
                A("vector", r2f2, waits=[("sMM", N("sMM"))], incs=[("sEV", 1)])
                free_at[("pt3", tc % 3)] = ("sEV", N("sEV"))
                r2_last_ev[(osp, tc)] = N("sEV")
            free_at[("wstr", osp % 2)] = ("sMM", N("sMM"))

        ffn_ev_end = N("sEV")
        if DEBUG:
            A("sync", lambda e: e.dma_start(out=dbg_r2, in_=r1s1),
              waits=[("sEV", ffn_ev_end)], incs=[("sAR", 16)])
            A("sync", lambda e: e.dma_start(out=dbg_ws, in_=w_str[1]),
              waits=[("sEV", ffn_ev_end)], incs=[("sAR", 16)])

        for tc in range(NTC):
            def ln2v(e, tc=tc):
                e.reduce_sum(out=stS[:], in_=r1s1[:, tc, :], axis=AX.X)
                e.tensor_mul(out=tmp2[:], in0=r1s1[:, tc, :], in1=r1s1[:, tc, :])
                e.reduce_sum(out=stS2[:], in_=tmp2[:], axis=AX.X)
                e.tensor_scalar_mul(out=stM[:], in0=stS[:], scalar1=1.0 / D)
                e.tensor_scalar_mul(out=stVA[:], in0=stS2[:], scalar1=1.0 / D)
                e.tensor_copy(out=stMI[:], in_=stM[:])
                e.tensor_copy(out=spcA[:], in_=spcB[:])
                e.tensor_mul(out=stM2[:], in0=stM[:], in1=stMI[:])
                e.tensor_copy(out=spcB[:], in_=spcA[:])
                return e.tensor_sub(out=stV[:], in0=stVA[:], in1=stM2[:])
            A("vector", ln2v, waits=[("sEV", r2_last_ev[(3, tc)]), ("sLN", N("sLN"))],
              incs=[("sEV", 1)])
            pre = N("sEV")

            A("scalar", lambda e: e.activation(
                out=stSD[:], in_=stV[:], func=AF.Sqrt, bias=eps_t[:]),
                waits=[("sEV", pre)], incs=[("sLN", 1)])
            sq_n = N("sLN")

            waits = [("sLN", sq_n)]
            if tc >= 1:
                waits.append(("sST", 16 * tc))

            def ln2ap(e, tc=tc):
                e.reciprocal(out=stIV[:], in_=stSD[:])
                e.tensor_copy(out=spcA[:], in_=spcB[:])
                e.tensor_mul(out=stMI[:], in0=stM[:], in1=stIV[:])
                e.tensor_copy(out=spcB[:], in_=spcA[:])
                e.tensor_scalar_mul(out=stNB[:], in0=stMI[:], scalar1=-1.0)
                e.tensor_copy(out=spcA[:], in_=spcB[:])
                return e.tensor_scalar(out=outbuf[:], in0=r1s1[:, tc, :],
                                       scalar1=stIV[:], scalar2=stNB[:],
                                       op0=ALU.mult, op1=ALU.add)
            A("vector", ln2ap, waits=waits, incs=[("sEV", 1)])

            A("sync", lambda e, tc=tc: e.dma_start(
                out=out[tc * TC:(tc + 1) * TC, :], in_=outbuf[:]),
                waits=[("sEV", N("sEV"))], incs=[("sST", 16)])

        sst_end = N("sST")
        A("gpsimd", lambda e: e.wait_ge(sems["sST"], sst_end) and None)
        A("sync", lambda e: e.wait_ge(sems["sST"], sst_end) and None)

        @block.sync
        def _(eng):
            sc.emit("sync", eng, sems)

        @block.gpsimd
        def _(eng):
            sc.emit("gpsimd", eng, sems)

        @block.tensor
        def _(eng):
            sc.emit("tensor", eng, sems)

        @block.vector
        def _(eng):
            sc.emit("vector", eng, sems)

        @block.scalar
        def _(eng):
            sc.emit("scalar", eng, sems)

    return nc


_NC_CACHE = None


def kernel(L0_attn_x, L0_attn_w_qkv, L0_attn_w_out, L0_ffn_factors, L0_ffn_bias):
    global _NC_CACHE
    x = np.ascontiguousarray(np.asarray(L0_attn_x, dtype=np.float32))
    w_qkv = np.ascontiguousarray(np.asarray(L0_attn_w_qkv, dtype=np.float32))
    w_out = np.ascontiguousarray(np.asarray(L0_attn_w_out, dtype=np.float32))
    factors = np.ascontiguousarray(np.asarray(L0_ffn_factors, dtype=np.float32))
    bias = np.ascontiguousarray(np.asarray(L0_ffn_bias, dtype=np.float32))

    if _NC_CACHE is None:
        _NC_CACHE = build_nc()
    nc = _NC_CACHE

    import ml_dtypes
    cb = np.zeros((128, 2308), np.float32)
    cb[:, 0:128] = np.eye(128, dtype=np.float32)
    cb[:, 128:256] = 1.0
    kk = np.arange(128)[:, None]
    qq = np.arange(QS)[None, :]
    for j in range(4):
        cb[:, 256 + j * QS:256 + (j + 1) * QS] = (qq >= kk + 128 * j).astype(np.float32)
    cb[:, 2304] = 1.0
    cblob = cb.astype(ml_dtypes.bfloat16)

    in_maps = []
    for c in range(8):
        b, g = c // 4, c % 4
        heads = list(range(HPC * g, HPC * g + HPC))
        cols = []
        for part in range(3):
            for h in heads:
                cols.append(w_qkv[:, part * D + h * DH:part * D + (h + 1) * DH])
        wq = np.ascontiguousarray(np.concatenate(cols, axis=1))
        in_maps.append({
            "xb": x[b],
            "xq": np.ascontiguousarray(x[b, g * TQ:(g + 1) * TQ]),
            "wqkv": wq,
            "wout": w_out,
            "factor": factors[c],
            "biasv": bias,
            "idx": np.array([g, 0, 0, 0], np.int32),
            "cblob": cblob,
        })

    res = run_bass_kernel_spmd(nc, in_maps, core_ids=list(range(8)))
    outp = np.empty((B, S, D), np.float32)
    for c in range(8):
        b, g = c // 4, c % 4
        outp[b, g * TQ:(g + 1) * TQ] = res.results[c]["out"]
    return outp


# version 19
# speedup vs baseline: 1.0285x; 1.0231x over previous
import contextlib
import numpy as np

import concourse.bass as bass
import concourse.mybir as mybir
from concourse.bass_utils import run_bass_kernel_spmd
from concourse.expressions import RuntimeValue

F32 = mybir.dt.float32
BF16 = mybir.dt.bfloat16
I32 = mybir.dt.int32
AF = mybir.ActivationFunctionType
ALU = mybir.AluOpType
AX = mybir.AxisListType

B, S, D, H, DH = 2, 4096, 2048, 16, 128
HPC = 4
WCOLS = 3 * HPC * DH
XS, NXS = 256, S // 256
QS, NQS = 512, S // 512
TQ = 1024
NCH = 4
NTC, TC = 8, 128
INV_SQRT_DH = float(1.0 / np.sqrt(128.0))
INV_SQRT_R = float(1.0 / np.sqrt(8.0))
EXP_SHIFT = -3.0
LN_EPS = 1e-5

DEBUG = False


class Sched:

    def __init__(self):
        self.count = {}
        self.ops = {e: [] for e in ("sync", "gpsimd", "tensor", "vector", "scalar")}

    def now(self, s):
        return self.count.get(s, 0)

    def add(self, eng, fn, waits=(), incs=()):
        self.ops[eng].append((fn, [w for w in waits if w is not None], list(incs)))
        for s, a in incs:
            self.count[s] = self.count.get(s, 0) + a
        return tuple((s, self.count[s]) for s, a in incs)

    def bump(self, s, a):
        self.count[s] = self.count.get(s, 0) + a

    def emit(self, eng_name, eng, sems):
        seen = {}
        for fn, waits, incs in self.ops[eng_name]:
            for s, v in waits:
                if v > 0 and seen.get(s, -1) < v:
                    eng.wait_ge(sems[s], v)
                    seen[s] = v
            inst = fn(eng)
            for s, a in incs:
                assert inst is not None, f"inc {s} on op with no instruction"
                inst = inst.then_inc(sems[s], a)


def build_nc():
    nc = bass.Bass("TRN2", target_bir_lowering=False)

    xb = nc.dram_tensor("xb", [S, D], F32, kind="ExternalInput").ap()
    xq = nc.dram_tensor("xq", [TQ, D], F32, kind="ExternalInput").ap()
    wqkv = nc.dram_tensor("wqkv", [D, WCOLS], F32, kind="ExternalInput").ap()
    wout = nc.dram_tensor("wout", [D, D], F32, kind="ExternalInput").ap()
    factor = nc.dram_tensor("factor", [D, D], F32, kind="ExternalInput").ap()
    biasv = nc.dram_tensor("biasv", [D], F32, kind="ExternalInput").ap()
    idx = nc.dram_tensor("idx", [4], I32, kind="ExternalInput").ap()
    cblob = nc.dram_tensor("cblob", [128, 2308], BF16, kind="ExternalInput").ap()
    out = nc.dram_tensor("out", [TQ, D], F32, kind="ExternalOutput").ap()
    if DEBUG:
        dbg_ao = nc.dram_tensor("dbg_ao", [128, HPC, S], BF16, kind="ExternalOutput").ap()
        dbg_r1 = nc.dram_tensor("dbg_r1", [128, NTC, D], F32, kind="ExternalOutput").ap()
        dbg_ag = nc.dram_tensor("dbg_ag", [128, 16, TQ], BF16, kind="ExternalOutput").ap()
        dbg_qf = nc.dram_tensor("dbg_qf", [128, HPC, S], BF16, kind="ExternalOutput").ap()
        dbg_kf = nc.dram_tensor("dbg_kf", [128, HPC, S], BF16, kind="ExternalOutput").ap()
        dbg_vt = nc.dram_tensor("dbg_vt", [128, 32, 512], BF16, kind="ExternalOutput").ap()
        dbg_s1 = nc.dram_tensor("dbg_s1", [128, NTC, D], F32, kind="ExternalOutput").ap()
        dbg_r2 = nc.dram_tensor("dbg_r2", [128, NTC, D], F32, kind="ExternalOutput").ap()
        dbg_sf = nc.dram_tensor("dbg_sf", [128, 16, TQ], BF16, kind="ExternalOutput").ap()
        dbg_ws = nc.dram_tensor("dbg_ws", [128, 16, 512], BF16, kind="ExternalOutput").ap()
        dbg_t2 = nc.dram_tensor("dbg_t2", [NTC, 128, D], F32, kind="ExternalOutput").ap()
        dbg_st = nc.dram_tensor("dbg_st", [NTC, 128, 16], F32, kind="ExternalOutput").ap()
        dbg_xq = nc.dram_tensor("dbg_xq", [NTC, 128, D], F32, kind="ExternalOutput").ap()
        dbg_agin = nc.dram_tensor("dbg_agin", [NCH, HPC, 128, TQ], BF16, kind="ExternalOutput").ap()

    ar_in = nc.dram_tensor("ar_in", [D * D], F32, kind="Internal").ap()
    ar_out = nc.dram_tensor("ar_out", [D * D], F32, kind="Internal").ap()
    ag_in = nc.dram_tensor("ag_in", [NCH, HPC, 128, TQ], BF16, kind="Internal").ap()
    ag_out = nc.dram_tensor("ag_out", [NCH, 4, HPC, 128, TQ], BF16, kind="Internal").ap()

    es = contextlib.ExitStack()
    with es:
        sbt = lambda nm, sh, dt: es.enter_context(nc.sbuf_tensor(nm, sh, dt))
        pst = lambda nm, sh, dt: es.enter_context(nc.psum_tensor(nm, sh, dt))

        RA = sbt("RA", [128, 49152], BF16)
        RB = sbt("RB", [128, 24576], BF16)
        RC = sbt("RC", [128, 16384], BF16)
        RD = sbt("RD", [128, 3072], BF16)
        RE = sbt("RE", [128, 8192], BF16)

        q_fm = RA[:, 0:16384].rearrange("p (h t) -> p h t", h=HPC)
        k_fm = RA[:, 16384:32768].rearrange("p (h t) -> p h t", h=HPC)
        v_tm = RA[:, 32768:49152].rearrange("p (c v) -> p c v", c=32)
        wqkv_sb = RB[:, 0:24576].rearrange("p (f c) -> p f c", f=16)
        x_tm = [RC[:, 0:4096].rearrange("p (u d) -> p u d", u=2),
                RC[:, 4096:8192].rearrange("p (u d) -> p u d", u=2)]
        x_fm1 = RC[:, 8192:12288].rearrange("p (f t) -> p f t", f=16)
        x_fm = [x_fm1, x_fm1]
        attnout_sl = RC[:, 12288:16384].rearrange("p (w h t) -> p w h t", w=2, h=HPC)
        pT = [RD[:, 0:512], RD[:, 512:1024], RD[:, 1024:1536], RD[:, 1536:2048],
              RD[:, 2048:2560], RE[:, 3072:3584]]
        bc_sb = RE[:, 2048:3072].bitcast(F32)
        tmpf = RD[:, 2048:3072].bitcast(F32)

        linv = RE[0:1, 2048:3072].bitcast(F32)
        bias_sb = RE[0:1, 4096:8192].bitcast(F32)

        r1s1 = RA[:, 0:32768].bitcast(F32).rearrange("p (c d) -> p c d", c=NTC)
        s1_fm = RA[:, 32768:49152].rearrange("p (f t) -> p f t", f=16)
        ag_sb = RB[:, 0:16384].rearrange("p (f t) -> p f t", f=16)
        s1b = RB[:, 16384:18432]
        outbuf = RB[:, 18432:22528].bitcast(F32)
        w_str = [RC[:, 0:8192].rearrange("p (f c) -> p f c", f=16),
                 RC[:, 8192:16384].rearrange("p (f c) -> p f c", f=16)]
        xq_sb = RE[:, 0:4096].bitcast(F32)
        tmp2 = RE[:, 4096:8192].bitcast(F32)

        CN = sbt("CN", [128, 260], BF16)
        ident = CN[:, 0:128]
        ones_sq = CN[:, 128:256]
        ones_k = CN[:, 256:257]
        masks = RE[:, 0:2048].rearrange("p (j q) -> p j q", j=4)
        ones_rf = sbt("ones_rf", [1, 128], F32)
        eps_t = sbt("eps_t", [128, 1], F32)
        expb_t = sbt("expb_t", [128, 1], F32)
        bias_bc = sbt("bias_bc", [128, D], F32)
        sidx = sbt("sidx", [1, 4], I32)
        stS = sbt("stS", [128, 1], F32)
        stS2 = sbt("stS2", [128, 1], F32)
        stM = sbt("stM", [128, 1], F32)
        stM2 = sbt("stM2", [128, 1], F32)
        stVA = sbt("stVA", [128, 1], F32)
        stV = sbt("stV", [128, 1], F32)
        stSD = sbt("stSD", [128, 1], F32)
        stIV = sbt("stIV", [128, 1], F32)
        stMI = sbt("stMI", [128, 1], F32)
        stNB = sbt("stNB", [128, 1], F32)
        spcA = sbt("spcA", [128, 8], F32)
        spcB = sbt("spcB", [128, 8], F32)

        PT = [pst("PT0", [128, 512], F32), pst("PT1", [128, 512], F32),
              pst("PT2", [128, 512], F32)]
        PA = pst("PA", [128, 512], F32)
        PL = pst("PL", [128, 512], F32)
        PQ = pst("PQ", [128, 512], F32)
        PX = [pst("PX0", [128, 512], BF16), pst("PX1", [128, 512], BF16)]

        semnames = ["sX0", "sX1", "sWQ", "sW0", "sW1", "sTR", "sFM", "sMM", "sEV",
                    "sEXP", "sMSK", "sAOS0", "sAOS1", "sAOS2", "sAOS3", "sCC",
                    "sAG", "sXQ", "sST", "sAR", "sBB", "sLN", "sIDX", "sDB", "sCB", "sCF"]
        sems = {n: es.enter_context(nc.semaphore(n)) for n in semnames}

        block = es.enter_context(nc.Block())

        sc = Sched()
        A, N = sc.add, sc.now
        free_at = {}

        FQ = D * D // 4
        for qi, qeng in enumerate(("sync", "scalar", "sync", "scalar")):
            A(qeng, lambda e, qi=qi: e.dma_start(
                out=ar_in[qi * FQ:(qi + 1) * FQ],
                in_=factor.rearrange("a b -> (a b)")[qi * FQ:(qi + 1) * FQ]),
                incs=[("sAR", 16)])
        A("sync", lambda e: e.dma_start(out=bias_sb, in_=biasv[None, :]),
          incs=[("sBB", 16)])
        A("gpsimd", lambda e: e.dma_start(out=sidx[:1], in_=idx[None, :]),
          incs=[("sIDX", 16)])

        def consts_fast(e):
            e.memset(eps_t[:], LN_EPS)
            e.memset(expb_t[:], EXP_SHIFT)
            return e.memset(ones_rf[:], 1.0)
        A("gpsimd", consts_fast, incs=[("sCF", 1)])
        def ldconsts(e):
            e.dma_start(out=CN[:, 0:256], in_=cblob[:, 0:256]).then_inc(sems["sCB"], 16)
            e.dma_start(out=CN[:, 256:257], in_=cblob[:, 2304:2305],
                        allow_slow_non_contiguous=True).then_inc(sems["sCB"], 16)
            return e.dma_start(out=RE[:, 0:2048], in_=cblob[:, 256:2304])
        A("gpsimd", ldconsts, incs=[("sCB", 16)])
        sc.count["sCB"] = sc.count.get("sCB", 0) + 32

        for qd in range(4):
            A("tensor", lambda e, qd=qd: e.matmul(
                out=PQ[:, :], lhsT=ones_rf[0:1, :],
                rhs=bias_sb[0:1, qd * 512:(qd + 1) * 512], start=True, stop=True),
                waits=[("sBB", 16), ("sCF", 1), ("sCB", 48), free_at.get("pq")], incs=[("sMM", 1)])
            ev = A("vector", lambda e, qd=qd: e.tensor_copy(
                out=bias_bc[:, qd * 512:(qd + 1) * 512], in_=PQ[:, :]),
                waits=[("sMM", N("sMM"))], incs=[("sEV", 1)])
            free_at["pq"] = ev[0]

        xevac_end = {}
        xfm_end = {}

        def emit_xload(t):
            sx = "sX0" if t % 2 == 0 else "sX1"
            A("gpsimd", lambda e, t=t: e.dma_start(
                out=x_tm[t % 2],
                in_=xb[t * XS:(t + 1) * XS, :].rearrange("(u p) d -> p u d", p=128)),
                waits=[free_at.get(("xtm", t % 2))], incs=[(sx, 16)])

        def emit_xgroup(t, gi):
            sx = "sX0" if t % 2 == 0 else "sX1"
            if gi < 8:
                grp = gi
                pxb = PX[grp % 2]

                def trf(e, t=t, grp=grp, pxb=pxb):
                    last = None
                    for i in range(4):
                        tile = grp * 4 + i
                        u, fc = tile // 16, tile % 16
                        last = e.transpose(out=pxb[:, i * 128:(i + 1) * 128],
                                           in_=x_tm[t % 2][:, u, fc * 128:(fc + 1) * 128],
                                           identity=ident[:])
                    return last
                A("tensor", trf,
                  waits=[(sx, 16 * (t // 2 + 1)), ("sBB", 16), ("sCF", 1), ("sCB", 48),
                         free_at.get(("px", grp % 2))],
                  incs=[("sTR", 1)])
                tile0 = grp * 4
                u, fc0 = tile0 // 16, tile0 % 16
                ev = A("vector", lambda e, t=t, u=u, fc0=fc0, pxb=pxb: e.tensor_copy(
                    out=x_fm[t % 2][:, fc0:fc0 + 4, u * 128:(u + 1) * 128],
                    in_=pxb[:].rearrange("p (i q) -> p i q", i=4)),
                    waits=[("sTR", N("sTR"))], incs=[("sFM", 1)])
                free_at[("px", grp % 2)] = ev[0]
                if grp == 7:
                    free_at[("xtm", t % 2)] = ("sTR", N("sTR"))
                    xfm_end[t] = N("sFM")
            elif gi < 16:
                cc = gi - 8

                def qkf(e, t=t, cc=cc):
                    last = None
                    for fc in range(16):
                        last = e.matmul(out=PQ[:, 0:XS],
                                        lhsT=wqkv_sb[:, fc, cc * 128:(cc + 1) * 128],
                                        rhs=x_fm[t % 2][:, fc, :],
                                        start=(fc == 0), stop=(fc == 15))
                    return last
                A("tensor", qkf,
                  waits=[("sFM", xfm_end[t]), ("sWQ", 16), free_at.get("pq")],
                  incs=[("sMM", 1)])
                dst = (q_fm if cc < 4 else k_fm)
                hh = cc % 4
                ev = A("vector", lambda e, t=t, dst=dst, hh=hh: e.tensor_copy(
                    out=dst[:, hh, t * XS:(t + 1) * XS], in_=PQ[:, 0:XS]),
                    waits=[("sMM", N("sMM"))], incs=[("sEV", 1)])
                free_at["pq"] = ev[0]
            else:
                u = gi - 16

                def vf(e, t=t, u=u):
                    last = None
                    for fc in range(16):
                        last = e.matmul(out=PQ[:, :],
                                        lhsT=x_fm[t % 2][:, fc, u * 128:(u + 1) * 128],
                                        rhs=wqkv_sb[:, fc, 1024:1536],
                                        start=(fc == 0), stop=(fc == 15))
                    return last
                A("tensor", vf,
                  waits=[("sFM", xfm_end[t]), ("sWQ", 16), free_at.get("pq")],
                  incs=[("sMM", 1)])
                ev = A("vector", lambda e, t=t, u=u: e.tensor_copy(
                    out=v_tm[:, 2 * t + u, :], in_=PQ[:, :]),
                    waits=[("sMM", N("sMM"))], incs=[("sEV", 1)])
                free_at["pq"] = ev[0]


        emit_xload(0)
        A("gpsimd", lambda e: e.dma_start(
            out=wqkv_sb, in_=wqkv.rearrange("(f p) c -> p f c", p=128)),
          incs=[("sWQ", 16)])
        emit_xload(1)
        for gi in range(18):
            emit_xgroup(0, gi)
        for gi in range(18):
            emit_xgroup(1, gi)

        LAG = 3
        chunks = []
        for s in range(NQS):
            for h in range(HPC):
                nkc = 4 * s + 4
                for kc in range(nkc):
                    chunks.append((s, h, kc, nkc))
        NCK = len(chunks)
        st_mm = {}
        gate_of = {}
        use_of = {}
        stg_done = {}

        def emit_epilogue(s, h):
            def rec(e):
                e.reciprocal(out=bc_sb[:], in_=PL[:, :])
                return e.tensor_copy(out=spcA[:], in_=spcB[:])
            A("vector", rec, waits=[("sMM", N("sMM"))], incs=[("sEV", 1)])
            free_at["pl"] = ("sEV", N("sEV"))
            waits = [("sMM", N("sMM"))]
            if s >= 2:
                waits.append(stg_done[s - 2])
            ev = A("vector", lambda e, s=s, h=h: e.tensor_mul(
                out=attnout_sl[:, s % 2, h, :], in0=PA[:, :], in1=bc_sb[:]),
                waits=waits, incs=[("sEV", 1)])
            free_at["pa2"] = ev[0]
            if h == HPC - 1:
                c, half = s // 2, s % 2
                saos = f"sAOS{c}"
                for f in range(HPC):
                    A("sync", lambda e, c=c, half=half, f=f, s=s: e.dma_start(
                        out=ag_in[c, f, :, half * QS:(half + 1) * QS],
                        in_=attnout_sl[:, s % 2, f, :]),
                        waits=[("sEV", N("sEV"))], incs=[(saos, 16)])
                stg_done[s] = (saos, N(saos))
                if half == 1:
                    A("gpsimd", lambda e, c=c: e.collective_compute(
                        "AllGather", ALU.bypass,
                        replica_groups=[[0, 1, 2, 3], [4, 5, 6, 7]],
                        ins=[ag_in[c]], outs=[ag_out[c]]),
                        waits=[(saos, N(saos))], incs=[("sCC", 1)])

        def emit_att_iter(n):
            if n >= LAG and n - LAG < NCK:
                m = n - LAG
                s, h, kc, nkc = chunks[m]
                use_i = use_of[m]
                waits = [gate_of[m]]
                if kc == 0:
                    waits += [free_at.get("pa2") or free_at.get("pa"),
                              free_at.get("pl")]

                def avf(e, s=s, h=h, kc=kc, use_i=use_i, nkc=nkc):
                    e.matmul(out=PA[:, :], lhsT=v_tm[:, kc, h * 128:(h + 1) * 128],
                             rhs=pT[use_i][:], start=(kc == 0), stop=(kc == nkc - 1))
                    return e.matmul(out=PL[:, :], lhsT=ones_sq[:, :],
                                    rhs=pT[use_i][:], start=(kc == 0),
                                    stop=(kc == nkc - 1))
                A("tensor", avf, waits=waits, incs=[("sMM", 1)])
                free_at[("pTb", use_i)] = ("sMM", N("sMM"))
                if kc >= 4 * s:
                    free_at[("pTb", m % 3)] = ("sMSK", N("sMSK"))
                if kc == nkc - 1:
                    emit_epilogue(s, h)

            if n < NCK:
                s, h, kc, nkc = chunks[n]
                ptb = PT[n % 3]
                A("tensor", lambda e, s=s, h=h, kc=kc, ptb=ptb: e.matmul(
                    out=ptb[:, :], lhsT=k_fm[:, h, kc * 128:(kc + 1) * 128],
                    rhs=q_fm[:, h, s * QS:(s + 1) * QS], start=True, stop=True),
                    waits=[("sEV", att_gate[s]), free_at.get(("pt2", n % 3))],
                    incs=[("sMM", 1)])
                st_mm[n] = N("sMM")

                pa_i = n % 3
                ex = A("scalar", lambda e, ptb=ptb, pa_i=pa_i: e.activation(
                    out=pT[pa_i][:], in_=ptb[:], func=AF.Exp,
                    bias=expb_t[:], scale=INV_SQRT_DH),
                    waits=[("sMM", st_mm[n]), ("sBB", 16), ("sCF", 1), ("sCB", 48),
                           free_at.get(("pTb", pa_i))],
                    incs=[("sEXP", 1)])
                free_at[("pt2", n % 3)] = ex[0]

                diag = kc >= 4 * s
                if diag:
                    j = kc - 4 * s
                    mb_i = 3 + (n % 3)
                    A("vector", lambda e, pa_i=pa_i, mb_i=mb_i, j=j: e.tensor_mul(
                        out=pT[mb_i][:], in0=pT[pa_i][:], in1=masks[:, j, :]),
                        waits=[("sEXP", N("sEXP")), ("sBB", 16), ("sCF", 1), ("sCB", 48),
                               free_at.get(("pTb", mb_i))],
                        incs=[("sMSK", 1)])
                    use_of[n] = mb_i
                    gate_of[n] = ("sMSK", N("sMSK"))
                else:
                    use_of[n] = pa_i
                    gate_of[n] = ("sEXP", N("sEXP"))

        att_gate = {}
        n = 0
        for s in range(NQS):
            if s == 1:
                A("gpsimd", lambda e: e.collective_compute(
                    "AllReduce", ALU.add, replica_groups=[[0, 1, 2, 3, 4, 5, 6, 7]],
                    ins=[ar_in], outs=[ar_out]),
                  waits=[("sAR", 64)], incs=[("sCC", 1)])
            att_gate[s] = xevac_end.get(2 * s + 1, N("sEV"))
            niter = HPC * (4 * s + 4)
            fill = []
            for t in (2 * s + 2, 2 * s + 3):
                if t < NXS:
                    fill.append(("load", t, None))
                    for gi in range(18):
                        fill.append(("grp", t, gi))
            done_f = 0
            COARSE = False
            for i in range(niter):
                want = len(fill) if COARSE else ((i + 1) * len(fill)) // niter
                while done_f < want:
                    kind, t, gi = fill[done_f]
                    if kind == "load":
                        emit_xload(t)
                    else:
                        emit_xgroup(t, gi)
                    done_f += 1
                emit_att_iter(n)
                n += 1
            if 2 * s + 1 + 2 < NXS:
                xevac_end[2 * s + 3] = N("sEV")
        for k in range(LAG):
            emit_att_iter(n)
            n += 1

        phase2_ev_end = N("sEV")
        phase2_msk_end = N("sMSK")



        def rdback(e):
            rg = e.alloc_register("rg")
            e.reg_load(rg, sidx[:1, 0:1])
            vg = RuntimeValue(rg, min_val=0, max_val=NCH - 1)
            e.wait_ge(sems["sCC"], vg + 2)
            last = None
            for r in range(4):
                for f in range(HPC):
                    last = e.dma_start(out=ag_sb[:, 4 * r + f, :],
                                       in_=ag_out[vg, r, f])
                    if not (r == 3 and f == HPC - 1):
                        last.then_inc(sems["sAG"], 16)
            return last
        A("gpsimd", rdback,
          waits=[("sIDX", 16)] + [(f"sAOS{c}", 128) for c in range(NCH)],
          incs=[("sAG", 16)])
        sc.count["sAG"] = 256

        if DEBUG:
            A("sync", lambda e: e.dma_start(out=dbg_ag, in_=ag_sb),
              waits=[("sAG", 256)], incs=[("sAR", 16)])
            A("sync", lambda e: e.dma_start(out=dbg_agin, in_=ag_in),
              waits=[(f"sAOS{c}", 128) for c in range(NCH)], incs=[("sAR", 16)])

        if DEBUG:
            for nm_, src_ in (("dbg_qf", None),):
                pass
            A("sync", lambda e: e.dma_start(out=dbg_qf, in_=q_fm),
              waits=[(f"sAOS{NCH-1}", 128)], incs=[("sAR", 16)])
            A("sync", lambda e: e.dma_start(out=dbg_kf, in_=k_fm),
              waits=[(f"sAOS{NCH-1}", 128)], incs=[("sAR", 16)])
            A("sync", lambda e: e.dma_start(out=dbg_vt, in_=v_tm),
              waits=[(f"sAOS{NCH-1}", 128)], incs=[("sAR", 16)])

        free_at[("pt3", 0)] = free_at.get(("pt2", 0))
        free_at[("pt3", 1)] = free_at.get(("pt2", 1))
        free_at[("pt3", 2)] = free_at.get(("pt2", 2))

        wout_last_ev = {}
        for osp in range(4):
            sw = f"sW{osp % 2}"
            A("gpsimd", lambda e, osp=osp: e.dma_start(
                out=w_str[osp % 2],
                in_=wout.rearrange("(f p) c -> p f c", p=128)[:, :, osp * 512:(osp + 1) * 512]),
                waits=[(f"sAOS{c}", 128) for c in range(NCH)]
                + [free_at.get(("wstr", osp % 2))],
                incs=[(sw, 16)])
            wn = (sw, N(sw))

            for tc in range(NTC):
                pt = PT[tc % 3]

                def woutf(e, osp=osp, tc=tc, pt=pt):
                    last = None
                    for fc in range(16):
                        last = e.matmul(out=pt[:, :],
                                        lhsT=ag_sb[:, fc, tc * 128:(tc + 1) * 128],
                                        rhs=w_str[osp % 2][:, fc, :],
                                        start=(fc == 0), stop=(fc == 15))
                    return last
                A("tensor", woutf,
                  waits=[("sAG", 256), wn, free_at.get(("pt3", tc % 3))],
                  incs=[("sMM", 1)])
                ev = A("vector", lambda e, osp=osp, tc=tc, pt=pt: e.tensor_copy(
                    out=r1s1[:, tc, osp * 512:(osp + 1) * 512], in_=pt[:, :]),
                    waits=[("sMM", N("sMM"))], incs=[("sEV", 1)])
                free_at[("pt3", tc % 3)] = ev[0]
                wout_last_ev[(osp, tc)] = N("sEV")
            free_at[("wstr", osp % 2)] = ("sMM", N("sMM"))

        wout_ev_end = N("sEV")

        if DEBUG:
            A("sync", lambda e: e.dma_start(out=dbg_r1, in_=r1s1),
              waits=[("sEV", wout_ev_end)], incs=[("sAR", 16)])

        s1fm_tc = {}
        r2_last_ev = {}
        for tc in range(NTC):
            A("sync", lambda e, tc=tc: e.dma_start(
                out=xq_sb, in_=xq[tc * TC:(tc + 1) * TC, :]),
                waits=[(f"sAOS{NCH - 1}", 128), free_at.get("xq")],
                incs=[("sXQ", 16)])

            lnw = [("sEV", wout_last_ev[(3, tc)]), ("sXQ", 16 * (tc + 1)), ("sLN", N("sLN"))]
            if DEBUG and tc >= 1:
                lnw.append(("sDB", 192 * tc))

            def ln1v(e, tc=tc):
                e.tensor_add(out=tmp2[:], in0=r1s1[:, tc, :], in1=xq_sb[:])
                e.tensor_mul(out=r1s1[:, tc, :], in0=tmp2[:], in1=tmp2[:])
                e.reduce_sum(out=stS[:], in_=tmp2[:], axis=AX.X)
                e.reduce_sum(out=stS2[:], in_=r1s1[:, tc, :], axis=AX.X)
                e.tensor_scalar_mul(out=stM[:], in0=stS[:], scalar1=1.0 / D)
                e.tensor_scalar_mul(out=stVA[:], in0=stS2[:], scalar1=1.0 / D)
                e.tensor_copy(out=stMI[:], in_=stM[:])
                e.tensor_copy(out=spcA[:], in_=spcB[:])
                e.tensor_mul(out=stM2[:], in0=stM[:], in1=stMI[:])
                e.tensor_copy(out=spcB[:], in_=spcA[:])
                return e.tensor_sub(out=stV[:], in0=stVA[:], in1=stM2[:])
            A("vector", ln1v, waits=lnw, incs=[("sEV", 1)])
            free_at["xq"] = ("sEV", N("sEV"))
            pre = N("sEV")
            if DEBUG:
                A("sync", lambda e, tc=tc: e.dma_start(out=dbg_t2[tc], in_=tmp2[:]),
                  waits=[("sEV", pre)], incs=[("sDB", 16)])
                A("sync", lambda e, tc=tc: e.dma_start(out=dbg_xq[tc], in_=xq_sb[:]),
                  waits=[("sEV", pre)], incs=[("sDB", 16)])

            A("scalar", lambda e: e.activation(
                out=stSD[:], in_=stV[:], func=AF.Sqrt, bias=eps_t[:]),
                waits=[("sEV", pre)], incs=[("sLN", 1)])
            sq_n = N("sLN")

            def ln1ap(e, tc=tc):
                e.reciprocal(out=stIV[:], in_=stSD[:])
                e.tensor_copy(out=spcA[:], in_=spcB[:])
                e.tensor_mul(out=stMI[:], in0=stM[:], in1=stIV[:])
                e.tensor_copy(out=spcB[:], in_=spcA[:])
                e.tensor_scalar_mul(out=stNB[:], in0=stMI[:], scalar1=-1.0)
                e.tensor_copy(out=spcA[:], in_=spcB[:])
                e.tensor_scalar(out=r1s1[:, tc, :], in0=tmp2[:],
                                scalar1=stIV[:], scalar2=stNB[:],
                                op0=ALU.mult, op1=ALU.add)
                return e.tensor_scalar_mul(out=s1b[:], in0=r1s1[:, tc, :],
                                           scalar1=INV_SQRT_R)
            lw = [("sLN", sq_n), free_at.get("s1b")]
            if DEBUG:
                lw.append(("sDB", 192 * tc + 32))
            A("vector", ln1ap, waits=lw, incs=[("sEV", 1)])
            if DEBUG:
                def dumpst(e, tc=tc):
                    last = None
                    for i, t_ in enumerate([stS, stS2, stM, stM2, stVA, stV, stSD, stIV, stMI, stNB]):
                        last = e.dma_start(out=dbg_st[tc, :, i:i + 1], in_=t_[:],
                                           allow_slow_non_contiguous=True)
                        if i < 9:
                            last.then_inc(sems["sDB"], 16)
                    return last
                A("sync", dumpst, waits=[("sEV", N("sEV"))], incs=[("sDB", 16)])
            s1b_ev = N("sEV")

            for grp in range(4):
                pxb = PX[grp % 2]

                def trs(e, tc=tc, grp=grp, pxb=pxb):
                    last = None
                    for i in range(4):
                        fc = grp * 4 + i
                        last = e.transpose(out=pxb[:, i * 128:(i + 1) * 128],
                                           in_=s1b[:, fc * 128:(fc + 1) * 128],
                                           identity=ident[:])
                    return last
                A("tensor", trs,
                  waits=[("sEV", s1b_ev), free_at.get(("px", grp % 2))],
                  incs=[("sTR", 1)])
                ev = A("vector", lambda e, tc=tc, grp=grp, pxb=pxb: e.tensor_copy(
                    out=s1_fm[:, grp * 4:grp * 4 + 4, tc * 128:(tc + 1) * 128],
                    in_=pxb[:].rearrange("p (i q) -> p i q", i=4)),
                    waits=[("sTR", N("sTR"))], incs=[("sFM", 1)])
                free_at[("px", grp % 2)] = ev[0]
            free_at["s1b"] = ("sTR", N("sTR"))
            s1fm_tc[tc] = N("sFM")
        s1fm_done = N("sFM")

        if DEBUG:
            A("sync", lambda e: e.dma_start(out=dbg_s1, in_=r1s1),
              waits=[("sFM", s1fm_done)], incs=[("sAR", 16)])
            A("sync", lambda e: e.dma_start(out=dbg_sf, in_=s1_fm),
              waits=[("sFM", s1fm_done)], incs=[("sAR", 16)])

        for osp in range(4):
            sw = f"sW{osp % 2}"
            A("gpsimd", lambda e, osp=osp: e.dma_start(
                out=w_str[osp % 2],
                in_=ar_out.rearrange("(f p c) -> p f c", p=128, f=16)
                [:, :, osp * 512:(osp + 1) * 512]),
                waits=[("sCC", 1), free_at.get(("wstr", osp % 2))],
                incs=[(sw, 16)])
            wn = (sw, N(sw))

            for tc in range(NTC):
                pt = PT[tc % 3]

                def ffnf(e, osp=osp, tc=tc, pt=pt):
                    last = None
                    for fc in range(16):
                        last = e.matmul(out=pt[:, :],
                                        lhsT=s1_fm[:, fc, tc * 128:(tc + 1) * 128],
                                        rhs=w_str[osp % 2][:, fc, :],
                                        start=(fc == 0), stop=(fc == 15))
                    return last
                A("tensor", ffnf,
                  waits=[("sFM", s1fm_tc[tc]), wn, free_at.get(("pt3", tc % 3))],
                  incs=[("sMM", 1)])

                def r2f(e, osp=osp, tc=tc, pt=pt):
                    e.tensor_add(out=tmpf[:], in0=pt[:, :],
                                 in1=bias_bc[:, osp * 512:(osp + 1) * 512])
                    return e.tensor_add(out=r1s1[:, tc, osp * 512:(osp + 1) * 512],
                                        in0=tmpf[:],
                                        in1=r1s1[:, tc, osp * 512:(osp + 1) * 512])
                def r2f2(e, osp=osp, tc=tc, pt=pt):
                    e.tensor_add(out=tmpf[:], in0=pt[:, :],
                                 in1=bias_bc[:, osp * 512:(osp + 1) * 512])
                    e.tensor_copy(out=spcA[:], in_=spcB[:])
                    e.tensor_add(out=tmp2[:, osp * 512:(osp + 1) * 512],
                                 in0=tmpf[:],
                                 in1=r1s1[:, tc, osp * 512:(osp + 1) * 512])
                    e.tensor_copy(out=spcB[:], in_=spcA[:])
                    return e.tensor_copy(out=r1s1[:, tc, osp * 512:(osp + 1) * 512],
                                         in_=tmp2[:, osp * 512:(osp + 1) * 512])
                A("vector", r2f2, waits=[("sMM", N("sMM"))], incs=[("sEV", 1)])
                free_at[("pt3", tc % 3)] = ("sEV", N("sEV"))
                r2_last_ev[(osp, tc)] = N("sEV")
            free_at[("wstr", osp % 2)] = ("sMM", N("sMM"))

        ffn_ev_end = N("sEV")
        if DEBUG:
            A("sync", lambda e: e.dma_start(out=dbg_r2, in_=r1s1),
              waits=[("sEV", ffn_ev_end)], incs=[("sAR", 16)])
            A("sync", lambda e: e.dma_start(out=dbg_ws, in_=w_str[1]),
              waits=[("sEV", ffn_ev_end)], incs=[("sAR", 16)])

        for tc in range(NTC):
            def ln2v(e, tc=tc):
                e.reduce_sum(out=stS[:], in_=r1s1[:, tc, :], axis=AX.X)
                e.tensor_mul(out=tmp2[:], in0=r1s1[:, tc, :], in1=r1s1[:, tc, :])
                e.reduce_sum(out=stS2[:], in_=tmp2[:], axis=AX.X)
                e.tensor_scalar_mul(out=stM[:], in0=stS[:], scalar1=1.0 / D)
                e.tensor_scalar_mul(out=stVA[:], in0=stS2[:], scalar1=1.0 / D)
                e.tensor_copy(out=stMI[:], in_=stM[:])
                e.tensor_copy(out=spcA[:], in_=spcB[:])
                e.tensor_mul(out=stM2[:], in0=stM[:], in1=stMI[:])
                e.tensor_copy(out=spcB[:], in_=spcA[:])
                return e.tensor_sub(out=stV[:], in0=stVA[:], in1=stM2[:])
            A("vector", ln2v, waits=[("sEV", r2_last_ev[(3, tc)]), ("sLN", N("sLN"))],
              incs=[("sEV", 1)])
            pre = N("sEV")

            A("scalar", lambda e: e.activation(
                out=stSD[:], in_=stV[:], func=AF.Sqrt, bias=eps_t[:]),
                waits=[("sEV", pre)], incs=[("sLN", 1)])
            sq_n = N("sLN")

            waits = [("sLN", sq_n)]
            if tc >= 1:
                waits.append(("sST", 16 * tc))

            def ln2ap(e, tc=tc):
                e.reciprocal(out=stIV[:], in_=stSD[:])
                e.tensor_copy(out=spcA[:], in_=spcB[:])
                e.tensor_mul(out=stMI[:], in0=stM[:], in1=stIV[:])
                e.tensor_copy(out=spcB[:], in_=spcA[:])
                e.tensor_scalar_mul(out=stNB[:], in0=stMI[:], scalar1=-1.0)
                e.tensor_copy(out=spcA[:], in_=spcB[:])
                return e.tensor_scalar(out=outbuf[:], in0=r1s1[:, tc, :],
                                       scalar1=stIV[:], scalar2=stNB[:],
                                       op0=ALU.mult, op1=ALU.add)
            A("vector", ln2ap, waits=waits, incs=[("sEV", 1)])

            A("sync", lambda e, tc=tc: e.dma_start(
                out=out[tc * TC:(tc + 1) * TC, :], in_=outbuf[:]),
                waits=[("sEV", N("sEV"))], incs=[("sST", 16)])

        sst_end = N("sST")
        A("gpsimd", lambda e: e.wait_ge(sems["sST"], sst_end) and None)
        A("sync", lambda e: e.wait_ge(sems["sST"], sst_end) and None)

        @block.sync
        def _(eng):
            sc.emit("sync", eng, sems)

        @block.gpsimd
        def _(eng):
            sc.emit("gpsimd", eng, sems)

        @block.tensor
        def _(eng):
            sc.emit("tensor", eng, sems)

        @block.vector
        def _(eng):
            sc.emit("vector", eng, sems)

        @block.scalar
        def _(eng):
            sc.emit("scalar", eng, sems)

    return nc


_NC_CACHE = None


def kernel(L0_attn_x, L0_attn_w_qkv, L0_attn_w_out, L0_ffn_factors, L0_ffn_bias):
    global _NC_CACHE
    x = np.ascontiguousarray(np.asarray(L0_attn_x, dtype=np.float32))
    w_qkv = np.ascontiguousarray(np.asarray(L0_attn_w_qkv, dtype=np.float32))
    w_out = np.ascontiguousarray(np.asarray(L0_attn_w_out, dtype=np.float32))
    factors = np.ascontiguousarray(np.asarray(L0_ffn_factors, dtype=np.float32))
    bias = np.ascontiguousarray(np.asarray(L0_ffn_bias, dtype=np.float32))

    if _NC_CACHE is None:
        _NC_CACHE = build_nc()
    nc = _NC_CACHE

    import ml_dtypes
    cb = np.zeros((128, 2308), np.float32)
    cb[:, 0:128] = np.eye(128, dtype=np.float32)
    cb[:, 128:256] = 1.0
    kk = np.arange(128)[:, None]
    qq = np.arange(QS)[None, :]
    for j in range(4):
        cb[:, 256 + j * QS:256 + (j + 1) * QS] = (qq >= kk + 128 * j).astype(np.float32)
    cb[:, 2304] = 1.0
    cblob = cb.astype(ml_dtypes.bfloat16)

    in_maps = []
    for c in range(8):
        b, g = c // 4, c % 4
        heads = list(range(HPC * g, HPC * g + HPC))
        cols = []
        for part in range(3):
            for h in heads:
                cols.append(w_qkv[:, part * D + h * DH:part * D + (h + 1) * DH])
        wq = np.ascontiguousarray(np.concatenate(cols, axis=1))
        in_maps.append({
            "xb": x[b],
            "xq": np.ascontiguousarray(x[b, g * TQ:(g + 1) * TQ]),
            "wqkv": wq,
            "wout": w_out,
            "factor": factors[c],
            "biasv": bias,
            "idx": np.array([g, 0, 0, 0], np.int32),
            "cblob": cblob,
        })

    res = run_bass_kernel_spmd(nc, in_maps, core_ids=list(range(8)))
    outp = np.empty((B, S, D), np.float32)
    for c in range(8):
        b, g = c // 4, c % 4
        outp[b, g * TQ:(g + 1) * TQ] = res.results[c]["out"]
    return outp
